# version 11 (final)
# baseline (speedup 1.0000x reference)
import math
from contextlib import ExitStack

import numpy as np
import ml_dtypes

import concourse.bass as bass
import concourse.mybir as mybir
from concourse.bass_utils import run_bass_kernel_spmd

F32 = mybir.dt.float32
BF16 = mybir.dt.bfloat16
AF = mybir.ActivationFunctionType
ALU = mybir.AluOpType
AX = mybir.AxisListType

D = 1024
T = 4096
NC = 8
NG = 8
GT = 512
NCH = D // 128
E0, U0 = 8, 4096
E1, U1 = 8, 512
DFF = 4096
H, HD, KVH = 16, 64, 4
WINDOW = 256
EPS = 1e-5


class S:

    def __init__(self, nc, st, name):
        self.h = st.enter_context(nc.semaphore(name))
        self.v = 0

    def inc(self, inst, n=1):
        inst.then_inc(self.h, n)
        self.v += n
        return (self.h, self.v)

    def tok(self):
        return (self.h, self.v)


def wt(eng, token):
    if token is not None:
        eng.wait_ge(token[0], token[1])


def _bf(x):
    return np.ascontiguousarray(np.asarray(x, np.float32).astype(ml_dtypes.bfloat16))


def _f32c(x):
    return np.ascontiguousarray(np.asarray(x, dtype=np.float32))


def host_prep(inputs):
    d = {k: np.asarray(v) for k, v in inputs.items()}
    w = {}
    w["wqT0"] = _f32c(d["L0_attn_q_proj_weight"].T)
    w["wkT0"] = _f32c(d["L0_attn_k_proj_weight"].T)
    w["wvT0"] = _f32c(d["L0_attn_v_proj_weight"].T)
    w["woT0"] = _f32c(d["L0_attn_o_proj_weight"].T)
    w["wr0T"] = _f32c(d["L0_ffn_w_router"].T)
    w["wupT"] = _f32c(d["L0_ffn_w_up"].transpose(0, 2, 1))
    w["bup"] = _f32c(d["L0_ffn_b_up"].reshape(E0, 32, 128).transpose(2, 0, 1))
    w["wdownT"] = _f32c(d["L0_ffn_w_down"].transpose(0, 2, 1))
    w["bdown"] = _f32c(d["L0_ffn_b_down"])
    for k in ["Wq", "Wk", "Wv", "Wo", "Wf1", "Wf2"]:
        w[k.lower() + "1"] = _f32c(d[f"L1_attn_{k}"])
    for k in ["bq", "bk", "bv", "bo", "bf1", "bf2", "g1", "b1", "g2", "b2"]:
        w[k + "1"] = _f32c(d[f"L1_attn_{k}"]).reshape(1, -1)
    qord = []
    for cch in range(8):
        qord += [cch, cch + 4] if cch < 4 else [cch + 4, cch + 8]
    perm_cols = np.concatenate([np.arange(h * 64, (h + 1) * 64) for h in qord])
    w["wq1"] = _f32c(w["wq1"][:, perm_cols])
    w["bq1"] = _f32c(w["bq1"][:, perm_cols])
    w["wo1"] = _f32c(w["wo1"][perm_cols, :])
    w["wr1T"] = _f32c(d["L1_ffn_w_router"].T)
    w["wfcT"] = _f32c(d["L1_ffn_w_fc"].transpose(0, 2, 1))
    w["wprojT"] = _f32c(d["L1_ffn_w_proj"].transpose(0, 2, 1))

    c = {}
    c["ident"] = _bf(np.eye(128, dtype=np.float32))
    c["ones1"] = _bf(np.ones((1, 512), dtype=np.float32))
    c["ones1f"] = _f32c(np.ones((1, 512), dtype=np.float32))
    kk = np.arange(128)[:, None]
    qq = np.arange(512)[None, :]
    c["masks_c"] = _bf(np.stack([(kk <= qq - 128 * dl) for dl in range(4)]))
    m2 = np.stack([(kk <= qq - 128 * dl) for dl in range(4)])
    c["masks_c2"] = _bf(np.concatenate([m2, m2], axis=2))
    c["masks_cb"] = _bf(np.where(np.concatenate([m2, m2], axis=2), 0.0, -1e9))
    mw = np.stack([((128 * r + qq - kk >= 0) & (128 * r + qq - kk < WINDOW))
                   for r in range(-3, 3)])
    c["masks_wb"] = _bf(np.where(mw, 0.0, -1e9))
    c["masks_w"] = _bf(np.stack([((128 * r + qq - kk >= 0) & (128 * r + qq - kk < WINDOW))
                                 for r in range(-3, 3)]))
    half = HD // 2
    theta = (1.0 / 10000.0 ** (np.arange(half, dtype=np.float64) / half))
    fr = np.outer(theta, np.arange(T, dtype=np.float64))
    cosr = np.cos(fr).astype(np.float32)
    sinr = np.sin(fr).astype(np.float32)
    c["ropeC"] = _f32c(np.concatenate([cosr, cosr, cosr, cosr], 0))
    c["ropeS"] = _f32c(np.concatenate([-sinr, sinr, -sinr, sinr], 0))
    perm = np.zeros((128, 128), np.float32)
    for r in range(128):
        base = (r // 64) * 64
        perm[(r - base + 32) % 64 + base, r] = 1.0
    c["permR"] = _bf(perm)
    c["identf"] = _f32c(np.eye(128, dtype=np.float32))
    return w, c


WEIGHT_SHAPES = {
    "wqT0": [D, D], "wkT0": [D, D], "wvT0": [D, D], "woT0": [D, D],
    "wr0T": [D, 8], "wupT": [E0, D, U0], "bup": [128, E0, 32],
    "wdownT": [E0, U0, D], "bdown": [E0, D],
    "wq1": [D, D], "wk1": [D, 256], "wv1": [D, 256], "wo1": [D, D],
    "wf11": [D, DFF], "wf21": [DFF, D],
    "bq1": [1, D], "bk1": [1, 256], "bv1": [1, 256], "bo1": [1, D],
    "bf11": [1, DFF], "bf21": [1, D],
    "g11": [1, D], "b11": [1, D], "g21": [1, D], "b21": [1, D],
    "wr1T": [D, 8], "wfcT": [E1, D, U1], "wprojT": [E1, U1, D],
}
CONST_SHAPES = {
    "ident": ([128, 128], BF16), "ones1": ([1, 512], BF16), "ones1f": ([1, 512], F32),
    "masks_c": ([4, 128, 512], BF16), "masks_c2": ([4, 128, 1024], BF16),
    "masks_cb": ([4, 128, 1024], BF16),
    "masks_w": ([6, 128, 512], BF16), "masks_wb": ([6, 128, 512], BF16),
    "ropeC": ([128, T], F32), "ropeS": ([128, T], F32), "permR": ([128, 128], BF16),
    "identf": ([128, 128], F32),
}
INTER_SHAPES = {
    "xnT": ([128, NCH * T], BF16),
    "w0": ([128, 32 * 8], F32),
    "xa": ([T, D], F32),
    "x1": ([T, D], F32),
    "xn1T": ([128, NCH * T], BF16),
    "m1": ([128, 32 * 8], F32),
    "xi": ([T, D], F32),
    "xab": ([T, D], F32),
}
PHASE_IN = {
    "A": ["x", "wr0T", "ident"],
    "B": ["x", "xnT", "wqT0", "wkT0", "wvT0", "woT0", "masks_c2", "ones1f"],
    "C": ["xa", "xnT", "w0", "wupT", "bup", "wdownT", "bdown", "ones1"],
    "D": ["x1", "wq1", "wk1", "wv1", "wo1", "bq1", "bk1", "bv1", "bo1",
          "g11", "b11", "wr1T", "ident", "masks_wb", "ropeC", "ropeS", "ones1",
          "ones1f", "permR", "identf"],
    "E": ["xi", "x1", "wf11", "wf21", "bf11", "bf21", "g21", "b21", "identf", "ones1"],
    "F": ["xab", "xn1T", "m1", "wfcT", "wprojT"],
}
PHASE_OUT = {
    "A": ["xnT", "w0"], "B": ["xa"], "C": ["x1"],
    "D": ["xi", "xn1T", "m1"], "E": ["xab"], "F": ["out"],
}
DEBUG_B = [False]


def declare_drams(nc, phases, fused):
    t = {}
    needed = set()
    for ph in phases:
        needed |= set(PHASE_IN[ph])
    if "x" in needed:
        t["x"] = nc.dram_tensor("x", [T, D], F32, kind="ExternalInput").ap()
    for n, shp in WEIGHT_SHAPES.items():
        if n in needed:
            t[n] = nc.dram_tensor(n, shp, F32, kind="ExternalInput").ap()
    for n, (shp, dt) in CONST_SHAPES.items():
        if n in needed:
            t[n] = nc.dram_tensor(n, shp, dt, kind="ExternalInput").ap()
    produced = set()
    for ph in phases:
        produced |= set(PHASE_OUT[ph])
    for n, (shp, dt) in INTER_SHAPES.items():
        if n in produced and n in needed:
            kind = "Internal" if fused else "ExternalOutput"
        elif n in produced:
            kind = "Internal" if fused else "ExternalOutput"
        elif n in needed:
            kind = "ExternalInput"
        else:
            continue
        t[n] = nc.dram_tensor(n, shp, dt, kind=kind).ap()
    if "E" in phases:
        t["dscr"] = nc.dram_tensor("dscr", [1, DFF], F32, kind="Internal").ap()
    if "F" in phases:
        t["out"] = nc.dram_tensor("out", [T, D], F32, kind="ExternalOutput").ap()
    if DEBUG_B[0]:
        t["dbg_ao"] = nc.dram_tensor("dbg_ao", [128, 1024], F32, kind="ExternalOutput").ap()
        t["dbg_pt"] = nc.dram_tensor("dbg_pt", [128, 2 * 1024], BF16, kind="ExternalOutput").ap()
        t["dbg_qt"] = nc.dram_tensor("dbg_qt", [128, 512], BF16, kind="ExternalOutput").ap()
        t["dbg_kt"] = nc.dram_tensor("dbg_kt", [128, 512], BF16, kind="ExternalOutput").ap()
        t["dbg_vp"] = nc.dram_tensor("dbg_vp", [128, 4 * 130], BF16, kind="ExternalOutput").ap()
    return t


def _xview(dram_ap, g, gt=GT):
    return dram_ap[g * gt:(g + 1) * gt, :].rearrange("(s p) d -> p s d", p=128)


def _xnT_view(dram_ap, g):
    return dram_ap.rearrange("p (c t) -> p c t", c=NCH)[:, :, g * GT:(g + 1) * GT]



def phase_A(nc, st, sems, t):
    sW, sMM, sEV, sVE, sGP = (sems[k] for k in ["ld0", "mm", "ev", "ve", "gp"])
    sX = [sems["ld1"], sems["ld2"]]
    sI = sems["ld3"]
    sSTp = [sems["st"], sems["st2"]]
    sW0 = sems["dbg1"]

    with ExitStack() as ph, nc.Block("phA") as blk:
        def sb(name, shape, dt):
            return ph.enter_context(nc.sbuf_tensor(name, shape, dt))

        xg = [sb(f"A_xg{i}", [128, 4 * 1024], F32) for i in range(2)]
        sq = [sb(f"A_sq{i}", [128, 1024], F32) for i in range(2)]
        ssum = sb("A_ssum", [128, 8], F32)
        negm = sb("A_negm", [128, 8], F32)
        ssq = sb("A_ssq", [128, 8], F32)
        lnv = sb("A_lnv", [128, 8], F32)
        rstd = sb("A_rstd", [128, 8], F32)
        mrstd = sb("A_mrstd", [128, 8], F32)
        xn = sb("A_xn", [128, 4 * 1024], BF16)
        xnT = [sb(f"A_xnT{i}", [128, 8 * 512], BF16) for i in range(2)]
        wr = sb("A_wr", [128, 8 * 8], BF16)
        lrt = sb("A_lrt", [128, 4 * 8], F32)
        m8 = sb("A_m8", [128, 4 * 8], F32)
        nm1 = sb("A_nm1", [128, 8], F32)
        m2c = sb("A_m2c", [128, 8], F32)
        msk = sb("A_msk", [128, 4 * 8], F32)
        p8 = sb("A_p8", [128, 4 * 8], F32)
        ws = sb("A_ws", [128, 8], F32)
        wiv = sb("A_wiv", [128, 8], F32)
        wfin = sb("A_wfin", [128, 32 * 8], F32)
        idt = sb("A_idt", [128, 128], BF16)
        epsb = sb("A_epsb", [128, 8], F32)
        psT = [ph.enter_context(nc.psum_tensor(f"A_psT{i}", [128, 512], BF16)) for i in range(2)]
        psR = ph.enter_context(nc.psum_tensor("A_psR", [128, 4 * 8], F32))

        tk_const = [None]

        tk_ident = [None]

        def _(sy):
            tk_ident[0] = sI.inc(sy.dma_start(out=idt[:], in_=t["ident"]), 16)
        blk.sync(_)

        def _(gp):
            tk_const[0] = sW.inc(
                gp.dma_start(out=wr.ap().rearrange("p (c e) -> p c e", c=8),
                             in_=t["wr0T"].rearrange("(c p) e -> p c e", p=128)), 16)
        blk.gpsimd(_)

        tk_eps = [None]

        def _(ve):
            tk_eps[0] = sVE.inc(ve.memset(epsb[:], EPS))
        blk.vector(_)

        tk_x = [None, None]
        tk_xn_done = [None] * NG
        tk_xnT_store = [None, None]

        for g in range(NG):
            b = g % 2

            def _(sy, g=g, b=b):
                if g >= 2:
                    wt(sy, tk_xn_done[g - 2])
                tk_x[b] = sX[b].inc(
                    sy.dma_start(out=xg[b].ap().rearrange("p (s d) -> p s d", s=4),
                                 in_=_xview(t["x"], g)), 16)
            blk.sync(_)

            tk_ln = [None] * 4
            for u in range(4):
                cs, ce = u * 1024, (u + 1) * 1024
                col = slice(2 * u, 2 * u + 1)

                def _(ve, b=b, cs=cs, ce=ce, col=col, u=u):
                    if u == 0:
                        wt(ve, tk_x[b])
                    ve.tensor_reduce(ssum[:, col], xg[b][:, cs:ce], AX.X, ALU.add)
                    ve.drain()
                    sVE.inc(ve.tensor_scalar(negm[:, col], ssum[:, col], -1.0 / D, None, ALU.mult))
                blk.vector(_)
                tvv = sVE.tok()

                def _(s, b=b, cs=cs, ce=ce, col=col, tvv=tvv):
                    wt(s, tvv)
                    s.activation(sq[b][:], xg[b][:, cs:ce], AF.Square,
                                 bias=negm[:, col], accum_out=ssq[:, col])
                    s.drain()
                    if tk_eps[0] is not None:
                        wt(s, tk_eps[0]); tk_eps[0] = None
                    s.activation(lnv[:, col], ssq[:, col], AF.Ln, bias=epsb[:, 0:1], scale=1.0 / D)
                    s.drain()
                    sEV.inc(s.activation(rstd[:, col], lnv[:, col], AF.Exp, scale=-0.5))
                blk.scalar(_)
                tss = sEV.tok()

                def _(ve, col=col, tss=tss):
                    wt(ve, tss)
                    ve.drain()
                    sVE.inc(ve.tensor_mul(mrstd[:, col], negm[:, col], rstd[:, col]))
                blk.vector(_)
                tmr = sVE.tok()

                def _(s, b=b, cs=cs, ce=ce, col=col, tmr=tmr):
                    wt(s, tmr)
                    sEV.inc(s.activation(xn[:, cs:ce], xg[b][:, cs:ce], AF.Identity,
                                         bias=mrstd[:, col], scale=rstd[:, col]))
                blk.scalar(_)
                tk_ln[u] = sEV.tok()
            tk_xn_done[g] = tk_ln[3]

            tk_tp = {}
            tk_cp = {}
            for u in range(4):
                for hf in range(2):
                    def _(tn, u=u, hf=hf):
                        wt(tn, tk_ln[u])
                        wt(tn, tk_ident[0])
                        wt(tn, tk_tbank[hf])
                        i = None
                        for cc in range(4):
                            ch = hf * 4 + cc
                            i = tn.transpose(psT[hf][:, cc * 128:(cc + 1) * 128],
                                             xn[:, u * 1024 + ch * 128: u * 1024 + (ch + 1) * 128],
                                             idt[:])
                        tk_tp[(u, hf)] = sMM.inc(i)
                    blk.tensor(_)

                    def _(ve, b=b, u=u, hf=hf):
                        wt(ve, tk_tp[(u, hf)])
                        if u == 0 and hf == 0:
                            wt(ve, tk_xnT_store[b])
                        dst = xnT[b].ap().rearrange("p (c q) -> p c q", c=8)[
                            :, hf * 4:(hf + 1) * 4, u * 128:(u + 1) * 128]
                        tk_cp[(u, hf)] = sVE.inc(
                            ve.tensor_copy(dst, psT[hf].ap().rearrange("p (c q) -> p c q", c=4)))
                        tk_tbank[hf] = tk_cp[(u, hf)]
                    blk.vector(_)

            tk_xnT_all = tk_cp[(3, 1)]

            tk_lr = [None] * 4
            for u in range(4):
                def _(tn, b=b, u=u):
                    wt(tn, tk_cp[(u, 0)])
                    wt(tn, tk_cp[(u, 1)])
                    if g == 0 and u == 0:
                        wt(tn, tk_const[0])
                    wt(tn, tk_lr_prev[0])
                    i = None
                    for cch in range(8):
                        i = tn.matmul(psR[:, u * 8:(u + 1) * 8],
                                      xnT[b][:, cch * 512 + u * 128: cch * 512 + u * 128 + 128],
                                      wr[:, cch * 8:(cch + 1) * 8],
                                      start=(cch == 0), stop=(cch == 7))
                    tk_lr[u] = sMM.inc(i)
                blk.tensor(_)

            def _(s, tk=list(tk_lr)):
                wt(s, tk[3])
                sEV.inc(s.activation(lrt[:], psR[:], AF.Copy))
            blk.scalar(_)
            tk_lrt = sEV.tok()
            tk_lr_prev[0] = tk_lrt

            def _(ve, tk_lrt=tk_lrt):
                wt(ve, tk_lrt)
                wt(ve, tk_router_done[0])
                i = None
                for u in range(4):
                    i = ve.max(out=m8[:, u * 8:(u + 1) * 8], in_=lrt[:, u * 8:(u + 1) * 8])
                sVE.inc(i)
            blk.vector(_)
            tk_m8 = sVE.tok()

            def _(gp, tk_m8=tk_m8):
                wt(gp, tk_m8)
                i = None
                for u in range(4):
                    gp.tensor_copy(m2c[:, 2 * u:2 * u + 1], m8[:, u * 8 + 1:u * 8 + 2])
                    gp.tensor_scalar(nm1[:, 2 * u:2 * u + 1], m8[:, u * 8:u * 8 + 1],
                                     -1.0, None, ALU.mult)
                    gp.drain()
                    i = gp.tensor_scalar(msk[:, u * 8:(u + 1) * 8], lrt[:, u * 8:(u + 1) * 8],
                                         m2c[:, 2 * u:2 * u + 1], None, ALU.is_ge)
                sGP.inc(i)
            blk.gpsimd(_)
            tk_msk = sGP.tok()

            def _(s, tk_msk=tk_msk):
                wt(s, tk_msk)
                i = None
                for u in range(4):
                    i = s.activation(p8[:, u * 8:(u + 1) * 8], lrt[:, u * 8:(u + 1) * 8],
                                     AF.Exp, bias=nm1[:, 2 * u:2 * u + 1])
                sEV.inc(i)
            blk.scalar(_)
            tk_p8 = sEV.tok()

            def _(ve, g=g, tk_p8=tk_p8):
                wt(ve, tk_p8)
                i = None
                for u in range(4):
                    sl = slice(u * 8, (u + 1) * 8)
                    ve.tensor_mul(p8[:, sl], p8[:, sl], msk[:, sl])
                    ve.drain()
                    ve.tensor_reduce(ws[:, 2 * u:2 * u + 1], p8[:, sl], AX.X, ALU.add)
                    ve.drain()
                    ve.reciprocal(wiv[:, 2 * u:2 * u + 1], ws[:, 2 * u:2 * u + 1])
                    ve.drain()
                    i = ve.tensor_scalar(wfin[:, (g * 4 + u) * 8:(g * 4 + u + 1) * 8],
                                         p8[:, sl], wiv[:, 2 * u:2 * u + 1], None, ALU.mult)
                sVE.inc(i)
            blk.vector(_)
            tk_router_done[0] = sVE.tok()

            def _(sy, g=g, b=b, tk_xnT_all=tk_xnT_all):
                wt(sy, tk_xnT_all)
                tk_xnT_store[b] = sSTp[b].inc(
                    sy.dma_start(out=_xnT_view(t["xnT"], g),
                                 in_=xnT[b].ap().rearrange("p (c q) -> p c q", c=8)), 16)
            blk.sync(_)

        def _(sy):
            wt(sy, tk_router_done[0])
            fin = sW0.inc(sy.dma_start(out=t["w0"], in_=wfin[:]), 16)
            wt(sy, fin)
            wt(sy, tk_xnT_store[0])
            wt(sy, tk_xnT_store[1])
        blk.sync(_)


tk_lr_prev = [None]
tk_router_done = [None]
tk_tbank = [None, None]
tk_psL_B = [None]
tk_epi_cons = {}
tk_epi_dbg = {}





def phase_B(nc, st, sems, t):
    sWp = [sems["ld0"], sems["ld5"]]
    sMM, sEV, sVE = (sems[k] for k in ["mm", "ev", "ve"])
    sX = [sems["ld1"], sems["ld2"]]
    sI = sems["ld3"]
    sW2 = sems["ld4"]
    sSTp = [sems["st"], sems["st2"]]

    aot_stack = ExitStack()
    AOT = [aot_stack.enter_context(nc.sbuf_tensor(f"B_aot{p}", [128, T], BF16))
           for p in range(8)]

    with ExitStack() as ph, nc.Block("phB1") as blk:
        def sb(name, shape, dt):
            return ph.enter_context(nc.sbuf_tensor(name, shape, dt))

        mks = sb("B_mks", [128, 4 * 1024], BF16)
        ones1f = sb("B_ones1f", [1, 512], F32)
        xnTg = [sb(f"B_xnTg{i}", [128, NCH * GT], BF16) for i in range(2)]
        wqp = [sb(f"B_wq{i}", [128, 8 * 128], BF16) for i in range(2)]
        wkp = [sb(f"B_wk{i}", [128, 8 * 128], BF16) for i in range(2)]
        wvp = [sb(f"B_wv{i}", [128, 8 * 128], BF16) for i in range(2)]
        QT = [sb(f"B_qt{i}", [128, T], BF16) for i in range(2)]
        KT = [sb(f"B_kt{i}", [128, T], BF16) for i in range(2)]
        Vp = [sb(f"B_vp{i}", [128, 32 * 130], BF16) for i in range(2)]
        PT = [sb(f"B_pt{r}", [128, 1024], BF16) for r in range(4)]
        ao = sb("B_ao", [128, 2 * 512], F32)
        linv = sb("B_linv", [1, 2 * 512], F32)
        lnl = sb("B_lnl", [1, 2 * 512], F32)

        psP = [ph.enter_context(nc.psum_tensor(f"B_psP{i}", [128, 512], F32)) for i in range(2)]
        psS = [ph.enter_context(nc.psum_tensor(f"B_psS{r}", [128, 1024], F32)) for r in range(2)]
        psV = [ph.enter_context(nc.psum_tensor(f"B_psV{h}", [65, 512], F32)) for h in range(2)]
        psL = psS[0]

        tk_res = [None]

        def _(sy):
            sI.inc(sy.dma_start(out=mks.ap().rearrange("p (m q) -> p m q", m=4),
                                in_=t["masks_c2"].rearrange("m p q -> p m q")), 16)
            tk_res[0] = sI.inc(sy.dma_start(out=ones1f[:], in_=t["ones1f"]), 16)
        blk.sync(_)

        def _(ve):
            i = None
            for bb in range(2):
                i = ve.memset(Vp[bb].ap().rearrange("p (k h c) -> p k h c", h=2, c=65)
                              [:, :, :, 64:65], 1.0)
            sVE.inc(i)
        blk.vector(_)
        tk_vones = sVE.tok()

        tk_w = [None, None]
        tk_projdone = [None, None]
        tk_flashdone = [None, None]
        tk_evP = {}
        tk_exp = {}
        tk_pvdone = {}
        tk_epi = {}
        tk_gproj = {}
        tk_xng = [None, None]

        for p in range(8):
            b = p % 2

            def _(gp, p=p, b=b):
                if p >= 2:
                    wt(gp, tk_projdone[b])
                for nm, buf in [("wqT0", wqp[b]), ("wkT0", wkp[b]), ("wvT0", wvp[b])]:
                    tk_w[b] = sWp[b].inc(gp.dma_start(
                        out=buf.ap().rearrange("p (c m) -> p c m", c=8),
                        in_=t[nm][:, p * 128:(p + 1) * 128].rearrange("(c pp) m -> pp c m",
                                                                      pp=128)), 16)
            blk.gpsimd(_)

            last_ev = None
            for g in range(NG):
                gb = g % 2

                def _(sy, p=p, g=g, gb=gb):
                    wt(sy, tk_gproj.get(gb))
                    tk_xng[gb] = sX[gb].inc(sy.dma_start(
                        out=xnTg[gb].ap().rearrange("p (c q) -> p c q", c=8),
                        in_=_xnT_view(t["xnT"], g)), 16)
                blk.sync(_)

                pmb = sMM.v
                peb = sEV.v

                def ev_tok(idx):
                    return (sEV.h, peb + idx + 1)

                def _(tn, g=g, gb=gb, p=p, b=b, pmb=pmb):
                    if g == 0:
                        wt(tn, tk_res[0])
                        wt(tn, tk_w[b])
                        if p >= 2:
                            wt(tn, tk_flashdone[b])
                    wt(tn, tk_xng[gb])
                    for idx in range(6):
                        bank = (g * 6 + idx) % 2
                        wt(tn, tk_evP.get(bank))
                        tk_evP[bank] = ev_tok(idx)
                        i = None
                        if idx == 0:
                            for c in range(NCH):
                                i = tn.matmul(psP[bank][:], wqp[b][:, c * 128:(c + 1) * 128],
                                              xnTg[gb][:, c * GT:(c + 1) * GT],
                                              start=(c == 0), stop=(c == NCH - 1))
                        elif idx == 1:
                            for c in range(NCH):
                                i = tn.matmul(psP[bank][:], wkp[b][:, c * 128:(c + 1) * 128],
                                              xnTg[gb][:, c * GT:(c + 1) * GT],
                                              start=(c == 0), stop=(c == NCH - 1))
                        else:
                            ts = idx - 2
                            for c in range(NCH):
                                i = tn.matmul(psP[bank][:, 0:128],
                                              xnTg[gb][:, c * GT + ts * 128: c * GT + (ts + 1) * 128],
                                              wvp[b][:, c * 128:(c + 1) * 128],
                                              start=(c == 0), stop=(c == NCH - 1))
                        sMM.inc(i)
                blk.tensor(_)

                def _(sc, g=g, b=b, pmb=pmb):
                    for idx in range(6):
                        bank = (g * 6 + idx) % 2
                        wt(sc, (sMM.h, pmb + idx + 1))
                        if idx == 0:
                            wt(sc, tk_vones)
                        if idx == 0:
                            i = sc.activation(QT[b][:, g * GT:(g + 1) * GT], psP[bank][:], AF.Copy)
                        elif idx == 1:
                            i = sc.activation(KT[b][:, g * GT:(g + 1) * GT], psP[bank][:], AF.Copy)
                        else:
                            kt = g * 4 + (idx - 2)
                            dst = Vp[b].ap().rearrange("p (k h c) -> p k h c", h=2, c=65)[
                                :, kt, :, 0:64]
                            i = sc.activation(dst, psP[bank][:, 0:128].rearrange(
                                "p (h c) -> p h c", h=2), AF.Copy)
                        last = sEV.inc(i)
                blk.scalar(_)
                last_ev = (sEV.h, sEV.v)
                tk_gproj[gb] = last_ev
            tk_projdone[b] = last_ev

            for g in range(NG):
                nkt = 4 * g + 4
                mb = sMM.v
                eb = sEV.v
                vb = sVE.v
                _mb = [mb]
                _nkt = [nkt]

                def S_tok(kt, mb=None, nkt=None):
                    return (sMM.h, _mb[0] + (1 if kt == 0 else 2 * kt))

                def PV_tok(kt):
                    return (sMM.h, _mb[0] + (2 * kt + 3 if kt < _nkt[0] - 1 else 2 * _nkt[0]))

                def exp_tok(kt):
                    return (sEV.h, eb + kt + 1)

                def mask_tok(kt, g=g):
                    dl = kt - 4 * g
                    return (sVE.h, vb + dl + 1)

                def pt_tok(kt, g=g):
                    return mask_tok(kt) if kt - 4 * g >= 0 else exp_tok(kt)

                def _(tn, g=g, nkt=nkt, b=b, mb=mb):
                    if g == 0:
                        wt(tn, tk_projdone[b])
                    for kt in range(nkt):
                        rot4 = kt % 4
                        if kt >= 2:
                            wt(tn, exp_tok(kt - 2))
                        if kt < 2 and tk_psL_B[0] is not None:
                            wt(tn, tk_psL_B[0])
                        i = None
                        for h in range(2):
                            pb = h * 64
                            i = tn.matmul(psS[kt % 2][:, h * 512:(h + 1) * 512],
                                          KT[b][pb:pb + 64, kt * 128:(kt + 1) * 128],
                                          QT[b][pb:pb + 64, g * GT:(g + 1) * GT],
                                          start=True, stop=True)
                        sMM.inc(i)
                        if kt >= 1:
                            ktp = kt - 1
                            wt(tn, pt_tok(ktp))
                            if ktp == 0:
                                wt(tn, tk_epi.get(0))
                                wt(tn, tk_epi.get(1))
                            i = None
                            for h in range(2):
                                i = tn.matmul(psV[h][:],
                                              Vp[b][:, ktp * 130 + h * 65: ktp * 130 + h * 65 + 65],
                                              PT[ktp % 4][:, h * 512:(h + 1) * 512],
                                              start=(ktp == 0), stop=False)
                            sMM.inc(i)
                    ktp = nkt - 1
                    wt(tn, pt_tok(ktp))
                    if ktp == 0:
                        wt(tn, tk_epi.get(0))
                        wt(tn, tk_epi.get(1))
                    i = None
                    for h in range(2):
                        i = tn.matmul(psV[h][:],
                                      Vp[b][:, ktp * 130 + h * 65: ktp * 130 + h * 65 + 65],
                                      PT[ktp % 4][:, h * 512:(h + 1) * 512],
                                      start=(ktp == 0), stop=True)
                    sMM.inc(i)
                blk.tensor(_)
                assert sMM.v == mb + 2 * nkt

                def _(sc, g=g, nkt=nkt, mb=mb):
                    for kt in range(nkt):
                        wt(sc, S_tok(kt))
                        if kt >= 4:
                            wt(sc, PV_tok(kt - 4))
                        sEV.inc(sc.activation(PT[kt % 4][:], psS[kt % 2][:],
                                              AF.Exp, scale=0.125))
                blk.scalar(_)
                assert sEV.v == eb + nkt

                def _(ve, g=g, nkt=nkt):
                    for kt in range(4 * g, nkt):
                        dl = kt - 4 * g
                        wt(ve, exp_tok(kt))
                        sVE.inc(ve.tensor_mul(PT[kt % 4][:], PT[kt % 4][:],
                                              mks[:, dl * 1024:(dl + 1) * 1024]))
                blk.vector(_)
                assert sVE.v == vb + 4

                tpv = PV_tok(nkt - 1)

                def _(sc, tpv=tpv):
                    wt(sc, tpv)
                    wt(sc, tk_epi_cons.get(0))
                    wt(sc, tk_epi_cons.get(1))
                    tk_epi[0] = sEV.inc(
                        sc.activation(ao[0:65, 0:512], psV[0][:], AF.Copy))
                    tk_epi[1] = sEV.inc(
                        sc.activation(ao[0:65, 512:1024], psV[1][:], AF.Copy))
                    sc.drain()
                    sc.activation(lnl[:, 0:512], ao[64:65, 0:512], AF.Ln)
                    sc.activation(lnl[:, 512:1024], ao[64:65, 512:1024], AF.Ln)
                    sc.drain()
                    sc.activation(linv[:, 0:512], lnl[:, 0:512], AF.Exp, scale=-1.0)
                    sEV.inc(sc.activation(linv[:, 512:1024], lnl[:, 512:1024],
                                          AF.Exp, scale=-1.0))
                blk.scalar(_)
                trec = sEV.tok()

                def _(tn, trec=trec, g=g):
                    wt(tn, trec)
                    if tk_psL_B[0] is not None:
                        wt(tn, tk_psL_B[0])
                    wt(tn, exp_tok(4 * g + 2))
                    wt(tn, mask_tok(4 * g + 3))
                    tn.matmul(psS[0][0:64, 0:512], ones1f[:, 0:64],
                              linv[:, 0:512], start=True, stop=True)
                    sMM.inc(tn.matmul(psS[1][0:64, 0:512], ones1f[:, 0:64],
                                      linv[:, 512:1024], start=True, stop=True))
                blk.tensor(_)
                tbl = sMM.tok()

                def _(ve, g=g, p=p, tbl=tbl):
                    wt(ve, tbl)
                    ve.tensor_mul(AOT[p][0:64, g * GT:(g + 1) * GT],
                                  ao[0:64, 0:512], psS[0][0:64, 0:512])
                    tk_psL_B[0] = sVE.inc(ve.tensor_mul(
                        AOT[p][64:128, g * GT:(g + 1) * GT],
                        ao[0:64, 512:1024], psS[1][0:64, 0:512]))
                    tk_epi_cons[0] = tk_psL_B[0]
                    tk_epi_cons[1] = tk_psL_B[0]
                blk.vector(_)
            tk_flashdone[b] = tk_psL_B[0]

    with ExitStack() as ph, nc.Block("phB2") as blk2:
        def sb(name, shape, dt):
            return ph.enter_context(nc.sbuf_tensor(name, shape, dt))

        woT = sb("B_woT", [128, 8 * 1024], BF16)
        xg = [sb(f"B_xg{i}", [128, 4 * 1024], F32) for i in range(2)]
        xa = [sb(f"B_xa{i}", [128, 4 * 1024], F32) for i in range(2)]
        psO = [ph.enter_context(nc.psum_tensor(f"B_psO{i}", [128, 512], F32)) for i in range(4)]

        tk_wo = [None]

        def _(gp):
            tk_wo[0] = sW2.inc(gp.dma_start(
                out=woT.ap().rearrange("p (c m) -> p c m", c=8),
                in_=t["woT0"].rearrange("(c pp) m -> pp c m", pp=128)), 16)
        blk2.gpsimd(_)

        tk_x = [None, None]
        tk_xa_store = [None, None]
        tk_psO = {}

        for g in range(NG):
            b = g % 2

            def _(sy, g=g, b=b):
                if g >= 2:
                    wt(sy, tk_xa_store[b])
                tk_x[b] = sX[b].inc(
                    sy.dma_start(out=xg[b].ap().rearrange("p (s d) -> p s d", s=4),
                                 in_=_xview(t["x"], g)), 16)
            blk2.sync(_)

            tk_add = {}
            for ts in range(4):
                for dh in range(2):
                    bank = (ts * 2 + dh) % 4

                    def _(tn, g=g, ts=ts, dh=dh, bank=bank):
                        if g == 0 and ts == 0 and dh == 0:
                            wt(tn, tk_wo[0])
                        wt(tn, tk_psO.get(bank))
                        i = None
                        for c in range(NCH):
                            i = tn.matmul(psO[bank][:],
                                          AOT[c][:, g * GT + ts * 128: g * GT + (ts + 1) * 128],
                                          woT[:, c * 1024 + dh * 512: c * 1024 + (dh + 1) * 512],
                                          start=(c == 0), stop=(c == NCH - 1))
                        sMM.inc(i)
                    blk2.tensor(_)
                    tmm = sMM.tok()

                    def _(ve, g=g, b=b, ts=ts, dh=dh, bank=bank, tmm=tmm):
                        wt(ve, tmm)
                        if ts == 0 and dh == 0:
                            wt(ve, tk_x[b])
                            if g >= 2:
                                wt(ve, tk_xa_store[b])
                        sl = slice(ts * 1024 + dh * 512, ts * 1024 + (dh + 1) * 512)
                        tk_psO[bank] = sVE.inc(
                            ve.tensor_add(xa[b][:, sl], xg[b][:, sl], psO[bank][:]))
                        tk_add[(ts, dh)] = tk_psO[bank]
                    blk2.vector(_)

            def _(sy, g=g, b=b):
                wt(sy, tk_add[(3, 1)])
                tk_xa_store[b] = sSTp[b].inc(
                    sy.dma_start(out=_xview(t["xa"], g),
                                 in_=xa[b].ap().rearrange("p (s d) -> p s d", s=4)), 16)
            blk2.sync(_)

        def _(sy):
            wt(sy, tk_xa_store[0])
            wt(sy, tk_xa_store[1])
        blk2.sync(_)
    aot_stack.close()



def phase_C(nc, st, sems, t):
    sWp = [sems["ld0"], sems["ld5"]]
    sMM, sEV, sVE = (sems[k] for k in ["mm", "ev", "ve"])
    sI = sems["ld3"]
    sXA = sems["ld1"]
    sST = sems["st"]

    with ExitStack() as ph, nc.Block("phC") as blk:
        def sb(name, shape, dt):
            return ph.enter_context(nc.sbuf_tensor(name, shape, dt))

        xnT = sb("C_xnT", [128, NCH * 2048], BF16)
        yac = sb("C_yac", [128, 16 * 1024], F32)
        w0 = sb("C_w0", [128, 32 * 8], F32)
        bup = sb("C_bup", [128, 8 * 32], F32)
        bdn = sb("C_bdn", [1, 8 * 1024], BF16)
        ones1 = sb("C_ones1", [1, 512], BF16)
        wup = [sb(f"C_wup{i}", [128, 8 * 1024], BF16) for i in range(2)]
        wdn = [sb(f"C_wdn{i}", [128, 8 * 1024], BF16) for i in range(2)]
        hsb = [sb(f"C_h{i}", [128, 8 * 512], BF16) for i in range(2)]

        psH = [ph.enter_context(nc.psum_tensor(f"C_psH{i}", [128, 512], F32)) for i in range(4)]
        psZ = [ph.enter_context(nc.psum_tensor(f"C_psZ{i}", [128, 512], F32)) for i in range(4)]

        tk_cst = [None]

        def _(sy):
            sI.inc(sy.dma_start(out=w0[:], in_=t["w0"]), 16)
            sI.inc(sy.dma_start(out=bup.ap().rearrange("p (e j) -> p e j", e=8),
                                in_=t["bup"]), 16)
            tk_cst[0] = sI.inc(sy.dma_start(out=ones1[:], in_=t["ones1"]), 16)
        blk.sync(_)

        tk_bdn = [None]

        def _(gp):
            tk_bdn[0] = sems["ld4"].inc(gp.dma_start(
                out=bdn.ap().rearrange("o (e d) -> o e d", e=8),
                in_=t["bdown"].rearrange("(o e) d -> o e d", o=1)), 16)
        blk.gpsimd(_)

        tk_xnT = [None]

        tk_w = [None, None]
        tk_wcons = [None, None]
        tk_yinit = [None]
        tk_ystore = [None]
        tk_h = {}
        tk_gpair = {}
        tk_hcons = {}
        tk_z = {}
        tk_gel = {}

        ji = 0
        for G in range(2):
            def _(sy, G=G):
                if G > 0:
                    wt(sy, tk_ystore[0])
                    wt(sy, tk_wcons[0])
                    wt(sy, tk_wcons[1])
                tk_yinit[0] = sXA.inc(
                    sy.dma_start(out=yac.ap().rearrange("p (u d) -> p u d", u=16),
                                 in_=_xview(t["xa"], G, gt=2048)), 16)
                tk_xnT[0] = sems["ld2"].inc(sy.dma_start(
                    out=xnT.ap().rearrange("p (c q) -> p c q", c=8),
                    in_=t["xnT"].rearrange("p (c t) -> p c t", c=NCH)
                    [:, :, G * 2048:(G + 1) * 2048]), 16)
            blk.sync(_)

            for e in range(E0):
                for sgi in range(4):
                    b = ji % 2

                    def _(gp, G=G, e=e, sgi=sgi, b=b):
                        if ji >= 2:
                            wt(gp, tk_wcons[b])
                        sWp[b].inc(gp.dma_start(
                            out=wup[b].ap().rearrange("p (c u) -> p c u", c=8),
                            in_=t["wupT"][e, :, sgi * 1024:(sgi + 1) * 1024]
                            .rearrange("(c p) u -> p c u", p=128)), 16)
                        tk_w[b] = sWp[b].inc(gp.dma_start(
                            out=wdn[b].ap().rearrange("p (uc d) -> p uc d", uc=8),
                            in_=t["wdownT"][e, sgi * 1024:(sgi + 1) * 1024, :]
                            .rearrange("(uc p) d -> p uc d", p=128)), 16)
                    blk.gpsimd(_)

                    for tpair in range(2):
                        for t5 in (tpair * 2, tpair * 2 + 1):
                            hr = t5 % 2
                            tok0 = t5 * 512
                            for uch in range(8):
                                hb = (t5 * 8 + uch) % 4

                                def _(tn, b=b, uch=uch, hb=hb, tok0=tok0, t5=t5):
                                    if uch == 0 and t5 % 2 == 0:
                                        wt(tn, tk_w[b])
                                        wt(tn, tk_xnT[0])
                                    wt(tn, tk_gel.get(hb))
                                    i = None
                                    for c in range(NCH):
                                        i = tn.matmul(
                                            psH[hb][:],
                                            wup[b][:, c * 1024 + uch * 128:
                                                   c * 1024 + (uch + 1) * 128],
                                            xnT[:, c * 2048 + tok0: c * 2048 + tok0 + 512],
                                            start=(c == 0), stop=(c == NCH - 1))
                                    sMM.inc(i)
                                blk.tensor(_)
                                tmm = sMM.tok()

                                def _(sc, e=e, sgi=sgi, uch=uch, hb=hb, hr=hr, tmm=tmm):
                                    wt(sc, tmm)
                                    wt(sc, tk_hcons.get(hr))
                                    if uch == 0 and tk_cst[0] is not None:
                                        wt(sc, tk_cst[0])
                                    tk_gel[hb] = sEV.inc(sc.activation(
                                        hsb[hr][:, uch * 512:(uch + 1) * 512], psH[hb][:],
                                        AF.Gelu, bias=bup[:, e * 32 + sgi * 8 + uch:
                                                          e * 32 + sgi * 8 + uch + 1]))
                                blk.scalar(_)
                            tk_gpair[t5 % 2] = (tk_gel[(t5 * 8 + 6) % 4], tk_gel[(t5 * 8 + 7) % 4])

                        for t5 in (tpair * 2, tpair * 2 + 1):
                            hr = t5 % 2
                            for tsub in range(4):
                                for dh in range(2):
                                    zb = (tsub * 2 + dh) % 4
                                    ti = G * 16 + t5 * 4 + tsub

                                    def _(tn, b=b, e=e, sgi=sgi, tsub=tsub, dh=dh, zb=zb,
                                          hr=hr, t5=t5):
                                        g01 = tk_gpair[t5 % 2]
                                        wt(tn, g01[0])
                                        wt(tn, g01[1])
                                        wt(tn, tk_z.get(zb))
                                        if sgi == 0 and tk_bdn[0] is not None:
                                            wt(tn, tk_bdn[0])
                                        i = None
                                        for uc in range(8):
                                            i = tn.matmul(
                                                psZ[zb][:],
                                                hsb[hr][:, uc * 512 + tsub * 128:
                                                         uc * 512 + (tsub + 1) * 128],
                                                wdn[b][:, uc * 1024 + dh * 512:
                                                       uc * 1024 + (dh + 1) * 512],
                                                start=(uc == 0), stop=(uc == 7 and sgi != 0))
                                        if sgi == 0:
                                            i = tn.matmul(
                                                psZ[zb][:], ones1[:, 0:128],
                                                bdn[:, e * 1024 + dh * 512:
                                                    e * 1024 + (dh + 1) * 512],
                                                start=False, stop=True)
                                        tk_wcons[b] = sMM.inc(i)
                                        tk_hcons[hr] = tk_wcons[b]
                                    blk.tensor(_)
                                    tmm = sMM.tok()

                                    def _(ve, e=e, tsub=tsub, dh=dh, zb=zb, ti=ti,
                                          t5=t5, tmm=tmm):
                                        wt(ve, tmm)
                                        if tk_yinit[0] is not None:
                                            wt(ve, tk_yinit[0])
                                        sl = slice((t5 * 4 + tsub) * 1024 + dh * 512,
                                                   (t5 * 4 + tsub) * 1024 + (dh + 1) * 512)
                                        tk_z[zb] = sVE.inc(ve.scalar_tensor_tensor(
                                            yac[:, sl], psZ[zb][:],
                                            w0[:, ti * 8 + e: ti * 8 + e + 1],
                                            yac[:, sl], ALU.mult, ALU.add))
                                        ve.drain()
                                    blk.vector(_)

                        if e == E0 - 1 and sgi == 3:
                            for t5 in (tpair * 2, tpair * 2 + 1):
                                def _(sy, G=G, t5=t5, tok=sVE.tok()):
                                    wt(sy, tok)
                                    tk_ystore[0] = sST.inc(sy.dma_start(
                                        out=t["x1"][G * 2048 + t5 * 512:
                                                    G * 2048 + (t5 + 1) * 512, :]
                                        .rearrange("(s p) d -> p s d", p=128),
                                        in_=yac.ap().rearrange("p (u d) -> p u d", u=16)
                                        [:, t5 * 4:(t5 + 1) * 4, :]), 16)
                                blk.sync(_)
                    ji += 1



        def _(sy):
            wt(sy, tk_ystore[0])
        blk.sync(_)




def phase_D(nc, st, sems, t):
    sMM, sEV, sVE = (sems[k] for k in ["mm", "ev", "ve"])
    sI = sems["ld3"]
    sIc = sems["ld4"]
    sX1 = sems["ld1"]
    sRP = sems["ld2"]
    sSTx = sems["st"]
    sSTn = sems["st2"]

    with ExitStack() as ph, nc.Block("phD") as blk:
        def sb(name, shape, dt):
            return ph.enter_context(nc.sbuf_tensor(name, shape, dt))

        wq = sb("D_wq", [128, 8 * 1024], BF16)
        wk = sb("D_wk", [128, 8 * 256], BF16)
        wv = sb("D_wv", [128, 8 * 256], BF16)
        wo = sb("D_wo", [128, 8 * 1024], BF16)
        wr = sb("D_wr", [128, 8 * 8], BF16)
        mks = sb("D_mks", [128, 6 * 512], BF16)
        idt = sb("D_idt", [128, 128], BF16)
        idtf = sb("D_idtf", [128, 128], F32)
        prm = sb("D_prm", [128, 128], BF16)
        ones1 = sb("D_ones1", [1, 512], BF16)
        ones1f = sb("D_ones1f", [1, 512], F32)
        g1c = sb("D_g1c", [128, 8], F32)
        b1c = sb("D_b1c", [128, 8], F32)
        b1cb = sb("D_b1cb", [128, 8], BF16)
        bqr = sb("D_bqr", [1, 1024], F32)
        bkr = sb("D_bkr", [1, 256], F32)
        bvr = sb("D_bvr", [1, 256], F32)
        bqe = sb("D_bqe", [1, 1024], BF16)
        bke = sb("D_bke", [1, 256], BF16)
        bve = sb("D_bve", [1, 256], BF16)
        boe = sb("D_boe", [1, 1024], BF16)
        epsb = sb("D_epsb", [128, 8], F32)
        KT1 = [sb(f"D_kt{i}", [128, 8 * 128], BF16) for i in range(2)]
        Vp1 = sb("D_vp", [128, 8 * 260], BF16)
        x1g = sb("D_x1g", [128, 4 * 1024], F32)
        xn1f = sb("D_xn1f", [128, 4 * 1024], F32)
        scr2 = sb("D_scr2", [128, 4 * 1024], F32)
        sq = [sb(f"D_sq{i}", [128, 1024], F32) for i in range(2)]
        ssum = sb("D_ssum", [128, 8], F32)
        negm = sb("D_negm", [128, 8], F32)
        ssq = sb("D_ssq", [128, 8], F32)
        lnv = sb("D_lnv", [128, 8], F32)
        rstd = sb("D_rstd", [128, 8], F32)
        mrstd = sb("D_mrstd", [128, 8], F32)
        xn1T = sb("D_xn1T", [128, 8 * 512], BF16)
        h1T = sb("D_h1T", [128, 8 * 512], BF16)
        ropC2 = [sb(f"D_ropC{i}", [128, 512], F32) for i in range(2)]
        ropS2 = [sb(f"D_ropS{i}", [128, 512], F32) for i in range(2)]
        scrT = sb("D_scrT", [128, 2 * 512], F32)
        QTb = sb("D_QTb", [128, 2 * 512], BF16)
        QT = sb("D_QT", [128, 8 * 512], BF16)
        AOT1 = sb("D_AOT1", [128, 8 * 512], BF16)
        PT = [sb(f"D_pt{r}", [128, 512], BF16) for r in range(4)]
        ao = sb("D_ao", [128, 2 * 512], F32)
        linv = sb("D_linv", [1, 2 * 512], F32)
        lnl = sb("D_lnl", [1, 2 * 512], F32)
        lrt = sb("D_lrt", [128, 4 * 8], F32)
        mx1 = sb("D_mx1", [128, 8], F32)
        m1f = sb("D_m1f", [128, 32 * 8], F32)
        xig = sb("D_xig", [128, 4 * 1024], F32)

        psA = [ph.enter_context(nc.psum_tensor(f"D_psA{i}", [128, 512], F32)) for i in range(2)]
        psT = [ph.enter_context(nc.psum_tensor(f"D_psT{i}", [128, 512], F32)) for i in range(2)]
        psS = [ph.enter_context(nc.psum_tensor(f"D_psS{i}", [128, 512], F32)) for i in range(2)]
        psV = [ph.enter_context(nc.psum_tensor(f"D_psV{i}", [65, 512], F32)) for i in range(2)]

        tk_cs = [None]
        tk_cg = [None]

        def _(sy):
            sI.inc(sy.dma_start(out=mks.ap().rearrange("p (m q) -> p m q", m=6),
                                in_=t["masks_wb"].rearrange("m p q -> p m q")), 16)
            sI.inc(sy.dma_start(out=idt[:], in_=t["ident"]), 16)
            sI.inc(sy.dma_start(out=idtf[:], in_=t["identf"]), 16)
            sI.inc(sy.dma_start(out=prm[:], in_=t["permR"]), 16)
            sI.inc(sy.dma_start(out=ones1[:], in_=t["ones1"]), 16)
            sI.inc(sy.dma_start(out=ones1f[:], in_=t["ones1f"]), 16)
            sI.inc(sy.dma_start(out=bqr[:], in_=t["bq1"]), 16)
            sI.inc(sy.dma_start(out=bkr[:], in_=t["bk1"]), 16)
            sI.inc(sy.dma_start(out=bvr[:], in_=t["bv1"]), 16)
            with nc.allow_non_contiguous_dma("tiny one-time 4KiB column loads"):
                sI.inc(sy.dma_start(out=g1c.ap().rearrange("p (c o) -> p c o", o=1),
                                    in_=t["g11"].rearrange("o (c p) -> p c o", p=128)), 16)
                tk_cs[0] = sI.inc(sy.dma_start(out=b1c.ap().rearrange("p (c o) -> p c o", o=1),
                                               in_=t["b11"].rearrange("o (c p) -> p c o", p=128)), 16)
        blk.sync(_)

        def _(gp):
            sIc.inc(gp.dma_start(out=wq.ap().rearrange("p (c m) -> p c m", c=8),
                                 in_=t["wq1"].rearrange("(c p) m -> p c m", p=128)), 16)
            sIc.inc(gp.dma_start(out=wk.ap().rearrange("p (c m) -> p c m", c=8),
                                 in_=t["wk1"].rearrange("(c p) m -> p c m", p=128)), 16)
            sIc.inc(gp.dma_start(out=wv.ap().rearrange("p (c m) -> p c m", c=8),
                                 in_=t["wv1"].rearrange("(c p) m -> p c m", p=128)), 16)
            sIc.inc(gp.dma_start(out=wo.ap().rearrange("p (c m) -> p c m", c=8),
                                 in_=t["wo1"].rearrange("(c p) m -> p c m", p=128)), 16)
            sIc.inc(gp.dma_start(out=wr.ap().rearrange("p (c e) -> p c e", c=8),
                                 in_=t["wr1T"].rearrange("(c p) e -> p c e", p=128)), 16)
            tk_cg[0] = sIc.inc(gp.dma_start(out=boe[:], in_=t["bo1"]), 16)
        blk.gpsimd(_)

        def _(ve):
            ve.memset(epsb[:], EPS)
            ve.memset(Vp1.ap().rearrange("p (k h c) -> p k h c", h=4, c=65)[:, :, :, 64:65], 1.0)
            wt(ve, tk_cs[0])
            wt(ve, tk_cg[0])
            sVE.inc(ve.tensor_copy(b1cb[:], b1c[:]))
        blk.vector(_)
        tk_pre_v = sVE.tok()

        tk_bias = [None]
        for nm, wbuf, br, be, n_out in [("q", wq, bqr, bqe, 1024), ("k", wk, bkr, bke, 256),
                                        ("v", wv, bvr, bve, 256)]:
            nhalf = (n_out + 511) // 512
            for hf in range(nhalf):
                n0, n1 = hf * 512, min((hf + 1) * 512, n_out)

                def _(tn, wbuf=wbuf, n0=n0, n1=n1, n_out=n_out):
                    wt(tn, tk_pre_v)
                    wt(tn, tk_bias[0])
                    i = None
                    for c in range(NCH):
                        i = tn.matmul(psA[0][:1, 0:n1 - n0], b1cb[:, c:c + 1],
                                      wbuf[:, c * n_out + n0: c * n_out + n1],
                                      start=(c == 0), stop=(c == NCH - 1))
                    sMM.inc(i)
                blk.tensor(_)
                tmm = sMM.tok()

                def _(ve, br=br, be=be, n0=n0, n1=n1, tmm=tmm):
                    wt(ve, tmm)
                    tk_bias[0] = sVE.inc(ve.tensor_add(be[:, n0:n1], br[:, n0:n1],
                                                       psA[0][0:1, 0:n1 - n0]))
                    ve.drain()
                blk.vector(_)

        def _(ve):
            wt(ve, tk_bias[0])
            i = None
            for wbuf, n_out in [(wq, 1024), (wk, 256), (wv, 256)]:
                for c in range(NCH):
                    i = ve.tensor_scalar(wbuf[:, c * n_out:(c + 1) * n_out],
                                         wbuf[:, c * n_out:(c + 1) * n_out],
                                         g1c[:, c:c + 1], None, ALU.mult)
                    ve.drain()
            sVE.inc(i)
        blk.vector(_)
        tk_wsc = sVE.tok()

        tk_x1 = [None]
        tk_rp = [None]
        tk_xi_store = [None]
        tk_xn1T_store = [None]
        tk_ln = {}
        tk_tbl = {}
        tk_m1v = [None]
        tk_opev = {}
        tk_psLd = [None]
        tk_ropeuse = [None]
        tk_ropeuse2 = [None, None]
        tk_aocp = {}
        tk_ttq = {}
        tk_fl2 = [None]
        tk_fl_hist = [None, None]

        for g in range(NG):
            def _(sy, g=g):
                if g > 0:
                    wt(sy, tk_ln[0])
                tk_x1[0] = sX1.inc(
                    sy.dma_start(out=x1g.ap().rearrange("p (s d) -> p s d", s=4),
                                 in_=_xview(t["x1"], g)), 16)
                if g > 1:
                    wt(sy, tk_ropeuse2[g % 2])
                sRP.inc(sy.dma_start(out=ropC2[g % 2][:],
                                     in_=t["ropeC"][:, g * GT:(g + 1) * GT]), 16)
                tk_rp[0] = sRP.inc(
                    sy.dma_start(out=ropS2[g % 2][:],
                                 in_=t["ropeS"][:, g * GT:(g + 1) * GT]), 16)
            blk.sync(_)
            ropC = ropC2[g % 2]
            ropS = ropS2[g % 2]

            for (src, dst, pass_i) in [(x1g, xn1f, 0), (xn1f, scr2, 1)]:
                def _(ve, src=src, pass_i=pass_i):
                    if pass_i == 0:
                        wt(ve, tk_x1[0])
                    else:
                        wt(ve, tk_ln[0])
                    i = None
                    for u in range(4):
                        col = slice(2 * u, 2 * u + 1)
                        ve.tensor_reduce(ssum[:, col], src[:, u * 1024:(u + 1) * 1024],
                                         AX.X, ALU.add)
                        ve.drain()
                        i = ve.tensor_scalar(negm[:, col], ssum[:, col], -1.0 / D,
                                             None, ALU.mult)
                        ve.drain()
                    sVE.inc(i)
                blk.vector(_)
                tvv = sVE.tok()

                def _(sc, src=src, tvv=tvv):
                    wt(sc, tvv)
                    i = None
                    for u in range(4):
                        col = slice(2 * u, 2 * u + 1)
                        sc.activation(sq[u % 2][:], src[:, u * 1024:(u + 1) * 1024],
                                      AF.Square, bias=negm[:, col], accum_out=ssq[:, col])
                        sc.drain()
                        sc.activation(lnv[:, col], ssq[:, col], AF.Ln,
                                      bias=epsb[:, 0:1], scale=1.0 / D)
                        sc.drain()
                        i = sc.activation(rstd[:, col], lnv[:, col], AF.Exp, scale=-0.5)
                        sc.drain()
                    sEV.inc(i)
                blk.scalar(_)
                tss = sEV.tok()

                def _(ve, tss=tss):
                    wt(ve, tss)
                    i = None
                    for u in range(4):
                        col = slice(2 * u, 2 * u + 1)
                        i = ve.tensor_mul(mrstd[:, col], negm[:, col], rstd[:, col])
                        ve.drain()
                    sVE.inc(i)
                blk.vector(_)
                tmr = sVE.tok()

                def _(sc, src=src, dst=dst, pass_i=pass_i, g=g, tmr=tmr):
                    wt(sc, tmr)
                    if pass_i == 1 and g > 0:
                        wt(sc, tk_tbl.get(1))
                    i = None
                    for u in range(4):
                        col = slice(2 * u, 2 * u + 1)
                        i = sc.activation(dst[:, u * 1024:(u + 1) * 1024],
                                          src[:, u * 1024:(u + 1) * 1024], AF.Identity,
                                          bias=mrstd[:, col], scale=rstd[:, col])
                    sEV.inc(i)
                blk.scalar(_)
                tk_ln[pass_i] = sEV.tok()

            for (src, dstT, ti) in [(xn1f, xn1T, 0), (scr2, h1T, 1)]:
                tmb = sMM.v
                tvb = sVE.v

                def _(tn, src=src, ti=ti, g=g, tvb=tvb):
                    wt(tn, tk_ln[ti])
                    if g == 0 and ti == 0:
                        wt(tn, tk_cs[0])
                    for u in range(4):
                        for hf in range(2):
                            idx = u * 2 + hf
                            wt(tn, (sVE.h, max(0, tvb + idx - 1)))
                            i = None
                            for cc in range(4):
                                ch = hf * 4 + cc
                                i = tn.transpose(psT[hf][:, cc * 128:(cc + 1) * 128],
                                                 src[:, u * 1024 + ch * 128:
                                                     u * 1024 + (ch + 1) * 128],
                                                 idtf[:])
                            sMM.inc(i)
                blk.tensor(_)

                def _(ve, dstT=dstT, ti=ti, tmb=tmb, g=g):
                    if ti == 0 and g > 0:
                        wt(ve, tk_xn1T_store[0])
                    for u in range(4):
                        for hf in range(2):
                            idx = u * 2 + hf
                            wt(ve, (sMM.h, tmb + idx + 1))
                            dst = dstT.ap().rearrange("p (c q) -> p c q", c=8)[
                                :, hf * 4:(hf + 1) * 4, u * 128:(u + 1) * 128]
                            sVE.inc(ve.tensor_copy(dst, psT[hf].ap()
                                                   .rearrange("p (c q) -> p c q", c=4)))
                blk.vector(_)
                tk_tbl[ti] = sVE.tok()

            def _(sy, g=g):
                wt(sy, tk_tbl[0])
                tk_xn1T_store[0] = sSTn.inc(
                    sy.dma_start(out=_xnT_view(t["xn1T"], g),
                                 in_=xn1T.ap().rearrange("p (c q) -> p c q", c=8)), 16)
            blk.sync(_)

            def _(tn, g=g):
                wt(tn, tk_tbl[0])
                if g > 0:
                    wt(tn, tk_opev.get(0))
                    wt(tn, tk_opev.get(1))
                else:
                    wt(tn, tk_bias[0])
                i = None
                for u in range(4):
                    for c in range(NCH):
                        i = tn.matmul(psA[0][:, u * 8:(u + 1) * 8],
                                      xn1T[:, c * 512 + u * 128: c * 512 + u * 128 + 128],
                                      wr[:, c * 8:(c + 1) * 8],
                                      start=(c == 0), stop=(c == NCH - 1))
                sMM.inc(i)
            blk.tensor(_)
            tlg = sMM.tok()

            def _(sc, tlg=tlg):
                wt(sc, tlg)
                sEV.inc(sc.activation(lrt[:], psA[0][:, 0:32], AF.Copy))
            blk.scalar(_)
            tlr = sEV.tok()

            def _(ve, g=g, tlr=tlr):
                wt(ve, tlr)
                i = None
                for u in range(4):
                    sl = slice(u * 8, (u + 1) * 8)
                    ve.tensor_reduce(mx1[:, 0:1], lrt[:, sl], AX.X, ALU.max)
                    ve.drain()
                    i = ve.tensor_scalar(m1f[:, (g * 4 + u) * 8:(g * 4 + u + 1) * 8],
                                         lrt[:, sl], mx1[:, 0:1], None, ALU.is_ge)
                    ve.drain()
                tk_m1v[0] = sVE.inc(i)
            blk.vector(_)

            tk_fl2[0] = tk_fl_hist[g % 2]

            peb = sEV.v
            pvb = sVE.v
            tok_proj = {}
            tok_perm = {}

            def _(tn, g=g, peb=peb, pvb=pvb):
                wt(tn, tk_wsc)
                wt(tn, tk_tbl[1])
                wt(tn, tk_m1v[0])

                def emit_proj(idx):
                    bank = idx % 2
                    if idx >= 2:
                        wt(tn, (sEV.h, peb + idx - 1))
                    i = None
                    if idx < 8:
                        for c in range(NCH):
                            i = tn.matmul(psA[bank][:],
                                          wq[:, c * 1024 + idx * 128: c * 1024 + (idx + 1) * 128],
                                          h1T[:, c * 512:(c + 1) * 512],
                                          start=(c == 0), stop=False)
                        i = tn.matmul(psA[bank][:], bqe[:, idx * 128:(idx + 1) * 128],
                                      ones1[:, 0:512], start=False, stop=True)
                    elif idx < 10:
                        ch = idx - 8
                        for c in range(NCH):
                            i = tn.matmul(psA[bank][:],
                                          wk[:, c * 256 + ch * 128: c * 256 + (ch + 1) * 128],
                                          h1T[:, c * 512:(c + 1) * 512],
                                          start=(c == 0), stop=False)
                        i = tn.matmul(psA[bank][:], bke[:, ch * 128:(ch + 1) * 128],
                                      ones1[:, 0:512], start=False, stop=True)
                    else:
                        ts = idx - 10
                        for c in range(NCH):
                            i = tn.matmul(psA[bank][:, 0:256],
                                          h1T[:, c * 512 + ts * 128: c * 512 + (ts + 1) * 128],
                                          wv[:, c * 256:(c + 1) * 256],
                                          start=(c == 0), stop=False)
                        i = tn.matmul(psA[bank][:, 0:256], ones1[:, 0:128], bve[:],
                                      start=False, stop=True)
                    tok_proj[idx] = sMM.inc(i)

                def emit_perm(j):
                    rot = j % 2
                    wt(tn, (sEV.h, peb + j + 1))
                    if j >= 2:
                        wt(tn, (sVE.h, pvb + (j - 2) + 1))
                    tok_perm[j] = sMM.inc(tn.matmul(psS[rot][:], prm[:],
                                                    QTb[:, rot * 512:(rot + 1) * 512],
                                                    start=True, stop=True))

                emit_proj(0)
                for idx in range(1, 14):
                    emit_proj(idx)
                    if idx - 1 < 10:
                        emit_perm(idx - 1)
                emit_perm(9)
            blk.tensor(_)

            def _(sc, g=g, pvb=pvb):
                for idx in range(14):
                    bank = idx % 2
                    wt(sc, tok_proj[idx])
                    if idx < 10:
                        rot = idx % 2
                        if idx >= 2:
                            wt(sc, (sVE.h, pvb + (idx - 2) + 1))
                        sEV.inc(sc.activation(QTb[:, rot * 512:(rot + 1) * 512],
                                              psA[bank][:], AF.Copy))
                    else:
                        kt = g * 4 + (idx - 10)
                        if idx == 10:
                            wt(sc, tk_fl2[0])
                        sEV.inc(sc.activation(
                            Vp1.ap().rearrange("p (k h c) -> p k h c", h=4, c=65)[
                                :, kt % 8, :, 0:64],
                            psA[bank][:, 0:256].rearrange("p (h c) -> p h c", h=4),
                            AF.Copy))
            blk.scalar(_)

            def _(ve, g=g):
                wt(ve, tk_rp[0])
                for idx in range(10):
                    rot = idx % 2
                    wt(ve, tok_perm[idx])
                    if idx < 8:
                        dst = QT[:, idx * 512:(idx + 1) * 512]
                    else:
                        if idx == 8:
                            wt(ve, tk_fl2[0])
                        ch4 = (g % 2) * 4
                        dst = KT1[idx - 8][:, ch4 * 128:(ch4 + 4) * 128]
                    ve.tensor_mul(scrT[:, 0:512], QTb[:, rot * 512:(rot + 1) * 512], ropC[:])
                    ve.tensor_mul(scrT[:, 512:1024], psS[rot][:], ropS[:])
                    ve.drain()
                    ve.tensor_add(dst, scrT[:, 0:512], scrT[:, 512:1024])
                    sVE.inc(ve.drain())
            blk.vector(_)
            tk_ropeuse[0] = sVE.tok()
            tk_ropeuse2[g % 2] = sVE.tok()
            tk_projg = sVE.tok()

            fmb = sMM.v
            feb = sEV.v
            fvb = sVE.v
            qinfo = []
            jq = {}
            j0c = 0
            for qh in range(16):
                kv = qh // 4
                qch = (qh % 4) + (4 if qh >= 8 else 0)
                kts = list(range(max(0, 4 * g - 2), 4 * g + 4))
                qinfo.append((qh, kv, qch, kts, j0c))
                for idx in range(len(kts)):
                    jq[j0c + idx] = (qh, idx)
                j0c += len(kts)
            nj = j0c
            tmb_of = {}
            off = fmb
            for (qh, kv, qch, kts, j0) in qinfo:
                tmb_of[qh] = (off, len(kts))
                off += 2 * len(kts) + 1
            cp_of = {}
            for (qh, kv, qch, kts, j0) in qinfo:
                cp_of[qh] = feb + j0 + len(kts) + qh + 1

            def S_tok_D(qh, idx):
                base, nk = tmb_of[qh]
                return (sMM.h, base + (idx + 1 if idx <= 1 else 2 * idx))

            def PV_tok_D(qh, idx):
                base, nk = tmb_of[qh]
                return (sMM.h, base + (2 * idx + 3 if idx < nk - 1 else 2 * nk))

            def exp_tok_D(j):
                return (sEV.h, feb + j + jq[j][0] + 1)

            def recip_tok(qh):
                return (sVE.h, fvb + 2 * qh + 1)

            def tt_tok(qh):
                return (sVE.h, fvb + 2 * qh + 2)

            def bc_tok(qh):
                base, nk = tmb_of[qh]
                return (sMM.h, base + 2 * nk + 1)

            psld_prev = tk_psLd[0]

            def _(tn, g=g):
                wt(tn, tk_projg)
                for (qh, kv, qch, kts, j0) in qinfo:
                    nk = len(kts)
                    pb = (kv % 2) * 64

                    def emit_S2(idx, qh=qh, kv=kv, qch=qch, kts=kts, j0=j0, pb=pb):
                        j = j0 + idx
                        kt = kts[idx]
                        rot = j % 2
                        if j >= 2:
                            wt(tn, exp_tok_D(j - 2))
                        mi = 4 * g - kt + 3
                        tn.matmul(psS[rot][:],
                                  KT1[kv // 2][pb:pb + 64, (kt % 8) * 128:(kt % 8) * 128 + 128],
                                  QT[:, qch * 512:(qch + 1) * 512][pb:pb + 64, :],
                                  start=True, stop=False)
                        sMM.inc(tn.matmul(psS[rot][:], idt[:],
                                          mks[:, mi * 512:(mi + 1) * 512],
                                          start=False, stop=True))

                    def emit_PV2(idx, qh=qh, kv=kv, kts=kts, j0=j0, nk=nk):
                        j = j0 + idx
                        kt = kts[idx]
                        wt(tn, exp_tok_D(j))
                        if idx == 0:
                            if qh >= 2:
                                wt(tn, tt_tok(qh - 2))
                            else:
                                wt(tn, tk_ttq.get(qh % 2))
                        sMM.inc(tn.matmul(psV[qh % 2][:],
                                          Vp1[:, (kt % 8) * 260 + kv * 65:
                                              (kt % 8) * 260 + kv * 65 + 65],
                                          PT[j % 4][:],
                                          start=(idx == 0), stop=(idx == nk - 1)))

                    emit_S2(0)
                    for idx in range(1, nk):
                        emit_S2(idx)
                        emit_PV2(idx - 1)
                    emit_PV2(nk - 1)
                    wt(tn, (sEV.h, cp_of[qh]))
                    sMM.inc(tn.matmul(psV[qh % 2][0:64, 0:512], ones1f[:, 0:64],
                                      linv[:, (qh % 2) * 512:(qh % 2 + 1) * 512],
                                      start=True, stop=True))
            blk.tensor(_)
            assert sMM.v == off

            def _(sc, g=g):
                for (qh, kv, qch, kts, j0) in qinfo:
                    nk = len(kts)
                    for idx in range(nk):
                        j = j0 + idx
                        wt(sc, S_tok_D(qh, idx))
                        if j >= 4:
                            wt(sc, PV_tok_D(*jq[j - 4]))
                        elif psld_prev is not None:
                            wt(sc, psld_prev)
                        sEV.inc(sc.activation(PT[j % 4][:], psS[j % 2][:],
                                              AF.Exp, scale=0.125))
                    wt(sc, PV_tok_D(qh, nk - 1))
                    if qh >= 2:
                        wt(sc, tt_tok(qh - 2))
                    else:
                        wt(sc, tk_ttq.get(qh % 2))
                    sc.activation(ao[0:65, (qh % 2) * 512:(qh % 2 + 1) * 512],
                                  psV[qh % 2][:], AF.Copy)
                    sc.drain()
                    sc.activation(lnl[:, (qh % 2) * 512:(qh % 2 + 1) * 512],
                                  ao[64:65, (qh % 2) * 512:(qh % 2 + 1) * 512], AF.Ln)
                    sc.drain()
                    sEV.inc(sc.activation(linv[:, (qh % 2) * 512:(qh % 2 + 1) * 512],
                                          lnl[:, (qh % 2) * 512:(qh % 2 + 1) * 512],
                                          AF.Exp, scale=-1.0))
                    assert sEV.v == cp_of[qh]
            blk.scalar(_)

            def _(ve, g=g):
                for (qh, kv, qch, kts, j0) in qinfo:
                    qhf = (qh // 4) % 2
                    ve.nop().then_inc(sVE.h, 1)
                    sVE.v += 1
                    wt(ve, bc_tok(qh))
                    sVE.inc(ve.tensor_mul(
                        AOT1[qhf * 64:(qhf + 1) * 64, qch * 512:(qch + 1) * 512],
                        ao[0:64, (qh % 2) * 512:(qh % 2 + 1) * 512],
                        psV[qh % 2][0:64, 0:512]))
                    ve.drain()
            blk.vector(_)
            assert sVE.v == fvb + 32
            tk_psLd[0] = (sVE.h, fvb + 32)
            tk_aocp[0] = (sEV.h, cp_of[14])
            tk_aocp[1] = (sEV.h, cp_of[15])
            tk_ttq[0] = tt_tok(14)
            tk_ttq[1] = tt_tok(15)
            tk_fl_hist[g % 2] = tk_psLd[0]

            omb = sMM.v
            ovb = sVE.v

            def _(tn, g=g, ovb=ovb):
                wt(tn, tk_psLd[0])
                for idx in range(8):
                    ts, dh = idx // 2, idx % 2
                    bank = idx % 2
                    if idx >= 2:
                        wt(tn, (sVE.h, ovb + idx - 1))
                    i = None
                    for c in range(NCH):
                        i = tn.matmul(psA[bank][:],
                                      AOT1[:, c * 512 + ts * 128: c * 512 + (ts + 1) * 128],
                                      wo[:, c * 1024 + dh * 512: c * 1024 + (dh + 1) * 512],
                                      start=(c == 0), stop=False)
                    i = tn.matmul(psA[bank][:], ones1[:, 0:128],
                                  boe[:, dh * 512:(dh + 1) * 512],
                                  start=False, stop=True)
                    sMM.inc(i)
            blk.tensor(_)

            def _(ve, g=g, omb=omb):
                for idx in range(8):
                    ts, dh = idx // 2, idx % 2
                    bank = idx % 2
                    wt(ve, (sMM.h, omb + idx + 1))
                    if idx == 0 and g > 0:
                        wt(ve, tk_xi_store[0])
                    sl = slice(ts * 1024 + dh * 512, ts * 1024 + (dh + 1) * 512)
                    sVE.inc(ve.tensor_add(xig[:, sl], xn1f[:, sl], psA[bank][:]))
                    ve.drain()
                tk_opev[0] = (sVE.h, sVE.v - 1)
                tk_opev[1] = (sVE.h, sVE.v)
            blk.vector(_)
            tk_xi_done = sVE.tok()

            def _(sy, g=g, tk_xi_done=tk_xi_done):
                wt(sy, tk_xi_done)
                tk_xi_store[0] = sSTx.inc(
                    sy.dma_start(out=_xview(t["xi"], g),
                                 in_=xig.ap().rearrange("p (s d) -> p s d", s=4)), 16)
            blk.sync(_)

        def _(sy):
            wt(sy, tk_m1v[0])
            fin = sems["dbg0"].inc(sy.dma_start(out=t["m1"], in_=m1f[:]), 16)
            wt(sy, fin)
            wt(sy, tk_xi_store[0])
            wt(sy, tk_xn1T_store[0])
        blk.sync(_)



def phase_E(nc, st, sems, t):
    sMM, sEV, sVE = (sems[k] for k in ["mm", "ev", "ve"])
    sI, sIc = sems["ld3"], sems["ld4"]
    sXI = sems["ld1"]
    sX1 = sems["ld2"]
    sSTp = [sems["st"], sems["st2"]]

    with ExitStack() as ph, nc.Block("phE") as blk:
        def sb(name, shape, dt):
            return ph.enter_context(nc.sbuf_tensor(name, shape, dt))

        wf1 = sb("E_wf1", [128, 8 * 4096], BF16)
        wf2 = sb("E_wf2", [128, 32 * 1024], BF16)
        g2c = sb("E_g2c", [128, 8], F32)
        b2c = sb("E_b2c", [128, 8], F32)
        b2cb = sb("E_b2cb", [128, 8], BF16)
        bf1row = sb("E_bf1row", [1, 4096], F32)
        bf1c = sb("E_bf1c", [128, 32], F32)
        bf2e = sb("E_bf2e", [1, 1024], BF16)
        idtf = sb("E_idtf", [128, 128], F32)
        ones1 = sb("E_ones1", [1, 512], BF16)
        epsb = sb("E_epsb", [128, 8], F32)
        xig = sb("E_xig", [128, 4 * 1024], F32)
        x1t = sb("E_x1t", [128, 2 * 1024], F32)
        lnscr = [sb(f"E_lnscr{i}", [128, 1024], F32) for i in range(2)]
        sq = lnscr
        ssum = sb("E_ssum", [128, 8], F32)
        negm = sb("E_negm", [128, 8], F32)
        ssq = sb("E_ssq", [128, 8], F32)
        lnv = sb("E_lnv", [128, 8], F32)
        rstd = sb("E_rstd", [128, 8], F32)
        mrstd = sb("E_mrstd", [128, 8], F32)
        h2T = sb("E_h2T", [128, 8 * 512], BF16)
        hf = sb("E_hf", [128, 32 * 256], BF16)

        psA = [ph.enter_context(nc.psum_tensor(f"E_psA{i}", [128, 512], F32)) for i in range(2)]
        psT = [ph.enter_context(nc.psum_tensor(f"E_psT{i}", [128, 512], F32)) for i in range(2)]
        psZ = [ph.enter_context(nc.psum_tensor(f"E_psZ{i}", [128, 512], F32)) for i in range(2)]
        psR = ph.enter_context(nc.psum_tensor("E_psR", [1, 512], F32))

        tk_cs = [None]
        tk_cg = [None]

        def _(sy):
            sI.inc(sy.dma_start(out=idtf[:], in_=t["identf"]), 16)
            sI.inc(sy.dma_start(out=bf1row[:], in_=t["bf11"]), 16)
            with nc.allow_non_contiguous_dma("tiny one-time column loads"):
                sI.inc(sy.dma_start(out=g2c.ap().rearrange("p (c o) -> p c o", o=1),
                                    in_=t["g21"].rearrange("o (c p) -> p c o", p=128)), 16)
                tk_cs[0] = sI.inc(sy.dma_start(out=b2c.ap().rearrange("p (c o) -> p c o", o=1),
                                               in_=t["b21"].rearrange("o (c p) -> p c o", p=128)), 16)
        blk.sync(_)

        def _(gp):
            sIc.inc(gp.dma_start(out=wf1.ap().rearrange("p (c m) -> p c m", c=8),
                                 in_=t["wf11"].rearrange("(c p) m -> p c m", p=128)), 16)
            sIc.inc(gp.dma_start(out=wf2.ap().rearrange("p (uc m) -> p uc m", uc=32),
                                 in_=t["wf21"].rearrange("(uc p) m -> p uc m", p=128)), 16)
            sIc.inc(gp.dma_start(out=ones1[:], in_=t["ones1"]), 16)
            tk_cg[0] = sIc.inc(gp.dma_start(out=bf2e[:], in_=t["bf21"]), 16)
        blk.gpsimd(_)

        def _(ve):
            ve.memset(epsb[:], EPS)
            wt(ve, tk_cs[0])
            wt(ve, tk_cg[0])
            sVE.inc(ve.tensor_copy(b2cb[:], b2c[:]))
        blk.vector(_)
        tk_pre = sVE.tok()

        tk_brow = [None]
        for hf8 in range(8):
            def _(tn, hf8=hf8):
                wt(tn, tk_pre)
                wt(tn, tk_brow[0])
                i = None
                for c in range(NCH):
                    i = tn.matmul(psR[:], b2cb[:, c:c + 1],
                                  wf1[:, c * 4096 + hf8 * 512: c * 4096 + (hf8 + 1) * 512],
                                  start=(c == 0), stop=(c == NCH - 1))
                sMM.inc(i)
            blk.tensor(_)
            tbm = sMM.tok()

            def _(ve, hf8=hf8, tbm=tbm):
                wt(ve, tbm)
                tk_brow[0] = sVE.inc(ve.tensor_add(
                    bf1row[:, hf8 * 512:(hf8 + 1) * 512],
                    bf1row[:, hf8 * 512:(hf8 + 1) * 512], psR[:]))
                ve.drain()
            blk.vector(_)

        tk_bounce = [None]

        def _(sy):
            wt(sy, tk_brow[0])
            tk_bounce[0] = sems["dbg0"].inc(sy.dma_start(out=t["dscr"], in_=bf1row[:]), 16)
        blk.sync(_)

        def _(sy):
            wt(sy, tk_bounce[0])
            with nc.allow_non_contiguous_dma("tiny bias column load"):
                tk_bounce[0] = sems["dbg1"].inc(
                    sy.dma_start(out=bf1c.ap().rearrange("p (j o) -> p j o", o=1),
                                 in_=t["dscr"].rearrange("o (j p) -> p j o", p=128)), 16)
        blk.sync(_)

        def _(ve):
            wt(ve, tk_brow[0])
            i = None
            for c in range(NCH):
                i = ve.tensor_scalar(wf1[:, c * 4096:(c + 1) * 4096],
                                     wf1[:, c * 4096:(c + 1) * 4096],
                                     g2c[:, c:c + 1], None, ALU.mult)
            sVE.inc(i)
        blk.vector(_)
        tk_wsc = sVE.tok()

        tk_xi = [None]
        tk_x1 = [None]
        tk_store = [None, None]
        tk_store_last = [None]
        tk_ln = {}
        tk_tb = {}
        tk_gel = {}
        tk_hcons = [None]
        tk_zc = {}

        for g in range(NG):
            def _(sy, g=g):
                if g > 0:
                    wt(sy, tk_zc.get(0))
                    wt(sy, tk_zc.get(1))
                tk_xi[0] = sXI.inc(
                    sy.dma_start(out=xig.ap().rearrange("p (s d) -> p s d", s=4),
                                 in_=_xview(t["xi"], g)), 16)
            blk.sync(_)

            def _(ve, g=g):
                wt(ve, tk_xi[0])
                i = None
                for u in range(4):
                    col = slice(2 * u, 2 * u + 1)
                    ve.tensor_reduce(ssum[:, col], xig[:, u * 1024:(u + 1) * 1024],
                                     AX.X, ALU.add)
                    ve.drain()
                    i = ve.tensor_scalar(negm[:, col], ssum[:, col], -1.0 / D, None, ALU.mult)
                    ve.drain()
                sVE.inc(i)
            blk.vector(_)
            tvv = sVE.tok()

            def _(sc, tvv=tvv):
                wt(sc, tvv)
                i = None
                for u in range(4):
                    col = slice(2 * u, 2 * u + 1)
                    sc.activation(sq[u % 2][:], xig[:, u * 1024:(u + 1) * 1024],
                                  AF.Square, bias=negm[:, col], accum_out=ssq[:, col])
                    sc.drain()
                    sc.activation(lnv[:, col], ssq[:, col], AF.Ln,
                                  bias=epsb[:, 0:1], scale=1.0 / D)
                    sc.drain()
                    i = sc.activation(rstd[:, col], lnv[:, col], AF.Exp, scale=-0.5)
                    sc.drain()
                sEV.inc(i)
            blk.scalar(_)
            tss = sEV.tok()

            def _(ve, tss=tss):
                wt(ve, tss)
                i = None
                for u in range(4):
                    col = slice(2 * u, 2 * u + 1)
                    i = ve.tensor_mul(mrstd[:, col], negm[:, col], rstd[:, col])
                    ve.drain()
                sVE.inc(i)
            blk.vector(_)
            tmr = sVE.tok()

            aeb = sEV.v
            tvb = sVE.v
            tmb = sMM.v

            def _(sc, g=g, tmr=tmr, tvb=tvb):
                wt(sc, tmr)
                for u in range(4):
                    col = slice(2 * u, 2 * u + 1)
                    if u >= 2:
                        wt(sc, (sVE.h, tvb + 2 * (u - 2) + 2))
                    sEV.inc(sc.activation(lnscr[u % 2][:], xig[:, u * 1024:(u + 1) * 1024],
                                          AF.Identity, bias=mrstd[:, col],
                                          scale=rstd[:, col]))
            blk.scalar(_)

            def _(tn, g=g, aeb=aeb, tvb=tvb):
                if g == 0:
                    wt(tn, tk_cs[0])
                for u in range(4):
                    wt(tn, (sEV.h, aeb + u + 1))
                    for hf2 in range(2):
                        idx = u * 2 + hf2
                        wt(tn, (sVE.h, max(0, tvb + idx - 1)))
                        i = None
                        for cc in range(4):
                            ch = hf2 * 4 + cc
                            i = tn.transpose(psT[hf2][:, cc * 128:(cc + 1) * 128],
                                             lnscr[u % 2][:, ch * 128:(ch + 1) * 128],
                                             idtf[:])
                        sMM.inc(i)
            blk.tensor(_)

            def _(ve, g=g, tmb=tmb):
                for u in range(4):
                    for hf2 in range(2):
                        idx = u * 2 + hf2
                        wt(ve, (sMM.h, tmb + idx + 1))
                        dst = h2T.ap().rearrange("p (c q) -> p c q", c=8)[
                            :, hf2 * 4:(hf2 + 1) * 4, u * 128:(u + 1) * 128]
                        sVE.inc(ve.tensor_copy(dst, psT[hf2].ap()
                                               .rearrange("p (c q) -> p c q", c=4)))
            blk.vector(_)
            tk_h2T = sVE.tok()

            for th in range(2):
                t0 = th * 256
                fmb = sMM.v
                geb = sEV.v

                def _(tn, th=th, t0=t0, geb=geb, g=g):
                    if th == 0:
                        wt(tn, tk_wsc)
                        wt(tn, tk_h2T)
                    for uch in range(32):
                        bank = uch % 2
                        if uch >= 2:
                            wt(tn, (sEV.h, geb + uch - 1))
                        elif tk_hcons[0] is not None:
                            wt(tn, tk_hcons[0])
                        i = None
                        for c in range(NCH):
                            i = tn.matmul(psA[bank][:, 0:256],
                                          wf1[:, c * 4096 + uch * 128:
                                              c * 4096 + (uch + 1) * 128],
                                          h2T[:, c * 512 + t0: c * 512 + t0 + 256],
                                          start=(c == 0), stop=(c == NCH - 1))
                        sMM.inc(i)
                blk.tensor(_)

                def _(sc, th=th, fmb=fmb, g=g):
                    for uch in range(32):
                        bank = uch % 2
                        wt(sc, (sMM.h, fmb + uch + 1))
                        if uch < 2 and tk_hcons[0] is not None:
                            wt(sc, tk_hcons[0])
                        if g == 0 and th == 0 and uch < 2:
                            wt(sc, tk_bounce[0])
                        sEV.inc(sc.activation(hf[:, uch * 256:(uch + 1) * 256],
                                              psA[bank][:, 0:256], AF.Gelu,
                                              bias=bf1c[:, uch:uch + 1]))
                blk.scalar(_)
                tk_f1 = sEV.tok()

                def _(sy, g=g, th=th):
                    wt(sy, tk_store_last[0])
                    tk_x1[0] = sX1.inc(sy.dma_start(
                        out=x1t.ap().rearrange("p (s d) -> p s d", s=2),
                        in_=t["x1"][g * GT + th * 256: g * GT + (th + 1) * 256, :]
                        .rearrange("(s p) d -> p s d", p=128)), 16)
                blk.sync(_)

                zmb = sMM.v
                zvb = sVE.v

                def _(tn, th=th, tk_f1=tk_f1, zvb=zvb):
                    wt(tn, tk_f1)
                    for idx in range(4):
                        tsub, dh = idx // 2, idx % 2
                        zb = idx % 2
                        if idx >= 2:
                            wt(tn, (sVE.h, zvb + (idx - 2) + 1))
                        else:
                            wt(tn, tk_zc.get(zb))
                        i = None
                        for uc in range(32):
                            i = tn.matmul(psZ[zb][:],
                                          hf[:, uc * 256 + tsub * 128:
                                             uc * 256 + (tsub + 1) * 128],
                                          wf2[:, uc * 1024 + dh * 512:
                                              uc * 1024 + (dh + 1) * 512],
                                          start=(uc == 0), stop=False)
                        i = tn.matmul(psZ[zb][:], ones1[:, 0:128],
                                      bf2e[:, dh * 512:(dh + 1) * 512],
                                      start=False, stop=True)
                        tk_hcons[0] = sMM.inc(i)
                blk.tensor(_)

                def _(ve, th=th, zmb=zmb):
                    for idx in range(4):
                        tsub, dh = idx // 2, idx % 2
                        zb = idx % 2
                        wt(ve, (sMM.h, zmb + idx + 1))
                        if idx == 0:
                            wt(ve, tk_x1[0])
                        slx = slice(tsub * 1024 + dh * 512, tsub * 1024 + (dh + 1) * 512)
                        sli = slice(th * 2048 + tsub * 1024 + dh * 512,
                                    th * 2048 + tsub * 1024 + (dh + 1) * 512)
                        ve.tensor_add(x1t[:, slx], x1t[:, slx], psZ[zb][:])
                        ve.drain()
                        tk_zc[zb] = sVE.inc(
                            ve.tensor_add(x1t[:, slx], x1t[:, slx], xig[:, sli]))
                        ve.drain()
                blk.vector(_)

                def _(sy, g=g, th=th):
                    wt(sy, tk_zc[0])
                    wt(sy, tk_zc[1])
                    tk_store[(g * 2 + th) % 2] = tk_store_last[0] = sSTp[(g * 2 + th) % 2].inc(
                        sy.dma_start(
                            out=t["xab"][g * GT + th * 256: g * GT + (th + 1) * 256, :]
                            .rearrange("(s p) d -> p s d", p=128),
                            in_=x1t.ap().rearrange("p (s d) -> p s d", s=2)), 16)
                blk.sync(_)

        def _(sy):
            wt(sy, tk_store[0])
            wt(sy, tk_store[1])
        blk.sync(_)



def phase_F(nc, st, sems, t):
    sMM, sEV, sVE = (sems[k] for k in ["mm", "ev", "ve"])
    sIc = sems["ld4"]
    sXB = sems["ld1"]
    sXN = sems["ld2"]
    sI = sems["ld3"]
    sST = sems["st"]

    with ExitStack() as ph, nc.Block("phF") as blk:
        def sb(name, shape, dt):
            return ph.enter_context(nc.sbuf_tensor(name, shape, dt))

        wfc = sb("F_wfc", [128, 8 * 4096], BF16)
        wpj = sb("F_wpj", [128, 8 * 4096], BF16)
        m1 = sb("F_m1", [128, 32 * 8], F32)
        xnTg = sb("F_xnTg", [128, 8 * 512], BF16)
        yac = sb("F_yac", [128, 4 * 1024], F32)
        scr = sb("F_scr", [128, 2 * 512], F32)
        hq = sb("F_hq", [128, 4 * 512], BF16)
        hs = sb("F_hs", [128, 2 * 512], BF16)

        psH = [ph.enter_context(nc.psum_tensor(f"F_psH{i}", [128, 512], F32)) for i in range(4)]
        psZ = [ph.enter_context(nc.psum_tensor(f"F_psZ{i}", [128, 512], F32)) for i in range(2)]

        tk_c = [None]

        def _(gp):
            sIc.inc(gp.dma_start(
                out=wfc.ap().rearrange("p (e c u) -> p e c u", e=8, c=8),
                in_=t["wfcT"].rearrange("e (c p) u -> p e c u", p=128)), 16)
            tk_c[0] = sIc.inc(gp.dma_start(
                out=wpj.ap().rearrange("p (e uc d) -> p e uc d", e=8, uc=4),
                in_=t["wprojT"].rearrange("e (uc p) d -> p e uc d", p=128)), 16)
        blk.gpsimd(_)

        tk_m = [None]

        def _(sy):
            tk_m[0] = sI.inc(sy.dma_start(out=m1[:], in_=t["m1"]), 16)
        blk.sync(_)

        tk_xn = [None]
        tk_y = [None]
        tk_ys = [None]
        tk_act = {}
        tk_hu = {}
        tk_z = {}
        tk_xnc = [None]
        tk_prjd = [None]

        for g in range(NG):
            def _(sy, g=g):
                if g > 0:
                    wt(sy, tk_ys[0])
                tk_y[0] = sXB.inc(
                    sy.dma_start(out=yac.ap().rearrange("p (s d) -> p s d", s=4),
                                 in_=_xview(t["xab"], g)), 16)
            blk.sync(_)

            def _(sy, g=g):
                if g > 0:
                    wt(sy, tk_xnc[0])
                tk_xn[0] = sXN.inc(
                    sy.dma_start(out=xnTg.ap().rearrange("p (c q) -> p c q", c=8),
                                 in_=_xnT_view(t["xn1T"], g)), 16)
            blk.sync(_)

            fmb_of = {}
            svb_of = {}
            seb_of = {}

            def emit_fc(e):
                fmb_of[e] = sMM.v
                svb_pred = sVE.v + (8 if e > 0 else 0)

                def _(tn, e=e):
                    if e == 0:
                        wt(tn, tk_xn[0])
                        wt(tn, tk_c[0])
                    for uch in range(4):
                        bank = uch
                        if e > 0:
                            wt(tn, (sVE.h, svb_of[e - 1] + uch + 1))
                        elif tk_act.get(bank) is not None:
                            wt(tn, tk_act[bank])
                        i = None
                        for c in range(NCH):
                            i = tn.matmul(psH[bank][:],
                                          wfc[:, e * 4096 + c * 512 + uch * 128:
                                              e * 4096 + c * 512 + (uch + 1) * 128],
                                          xnTg[:, c * 512:(c + 1) * 512],
                                          start=(c == 0), stop=(c == NCH - 1))
                        sMM.inc(i)
                blk.tensor(_)

            def emit_act(e):
                seb = sEV.v
                svb = sVE.v
                seb_of[e] = seb
                svb_of[e] = svb
                fmb = fmb_of[e]

                def _(sc, e=e, fmb=fmb, svb=svb, seb=seb):
                    for uch in range(4):
                        sb2 = uch % 2
                        wt(sc, (sMM.h, fmb + uch + 1))
                        if uch >= 2:
                            wt(sc, (sVE.h, svb + (uch - 2) + 1))
                        elif e > 0:
                            wt(sc, (sVE.h, svb_of[e - 1] + uch + 3))
                        sEV.inc(sc.activation(scr[:, sb2 * 512:(sb2 + 1) * 512],
                                               psH[uch][:], AF.Abs, scale=0.25))
                        sc.drain()
                    for uch in range(4):
                        sb2 = uch % 2
                        wt(sc, (sVE.h, svb + uch + 1))
                        wt(sc, tk_prjd[0])
                        sEV.inc(sc.activation(hq[:, uch * 512:(uch + 1) * 512],
                                              hs[:, sb2 * 512:(sb2 + 1) * 512],
                                              AF.Square))
                        sc.drain()
                blk.scalar(_)

                def _(ve, e=e, fmb=fmb, seb=seb):
                    for uch in range(4):
                        sb2 = uch % 2
                        wt(ve, (sEV.h, seb + uch + 1))
                        if uch >= 2:
                            wt(ve, (sEV.h, seb + 4 + (uch - 2) + 1))
                        elif e > 0:
                            wt(ve, (sEV.h, seb_of[e - 1] + 4 + uch + 3))
                        tk_act[uch] = sVE.inc(ve.scalar_tensor_tensor(
                            hs[:, sb2 * 512:(sb2 + 1) * 512], psH[uch][:], 0.75,
                            scr[:, sb2 * 512:(sb2 + 1) * 512], ALU.mult, ALU.add))
                        ve.drain()
                blk.vector(_)

            def emit_proj(e):
                tk_hq = (sEV.h, seb_of[e] + 8)
                pmb = sMM.v
                pvb = sVE.v

                def _(tn, e=e, tk_hq=tk_hq, pvb=pvb):
                    wt(tn, tk_hq)
                    for idx in range(8):
                        tsub, dh = idx // 2, idx % 2
                        zb = idx % 2
                        if idx >= 2:
                            wt(tn, (sVE.h, pvb + (idx - 2) + 1))
                        else:
                            wt(tn, tk_z.get(zb))
                        i = None
                        for uc in range(4):
                            i = tn.matmul(psZ[zb][:],
                                          hq[:, uc * 512 + tsub * 128:
                                             uc * 512 + (tsub + 1) * 128],
                                          wpj[:, e * 4096 + uc * 1024 + dh * 512:
                                              e * 4096 + uc * 1024 + (dh + 1) * 512],
                                          start=(uc == 0), stop=(uc == 3))
                        sMM.inc(i)
                blk.tensor(_)
                tk_prjd[0] = sMM.tok()

                def _(ve, e=e, g=g, pmb=pmb):
                    for idx in range(8):
                        tsub, dh = idx // 2, idx % 2
                        zb = idx % 2
                        wt(ve, (sMM.h, pmb + idx + 1))
                        if e == 0 and idx == 0:
                            wt(ve, tk_y[0])
                            wt(ve, tk_m[0])
                        sl = slice(tsub * 1024 + dh * 512, tsub * 1024 + (dh + 1) * 512)
                        ti = g * 4 + tsub
                        tk_z[zb] = sVE.inc(ve.scalar_tensor_tensor(
                            yac[:, sl], psZ[zb][:],
                            m1[:, ti * 8 + e: ti * 8 + e + 1],
                            yac[:, sl], ALU.mult, ALU.add))
                        ve.drain()
                blk.vector(_)

            emit_fc(0)
            for e in range(E1):
                emit_act(e)
                if e + 1 < E1:
                    emit_fc(e + 1)
                emit_proj(e)
                if e == E1 - 1:
                    tk_xnc[0] = tk_prjd[0]

            def _(sy, g=g):
                wt(sy, tk_z[0])
                wt(sy, tk_z[1])
                tk_ys[0] = sST.inc(
                    sy.dma_start(out=_xview(t["out"], g),
                                 in_=yac.ap().rearrange("p (s d) -> p s d", s=4)), 16)
            blk.sync(_)

        def _(sy):
            wt(sy, tk_ys[0])
        blk.sync(_)


PHASE_BUILDERS = {"A": phase_A, "B": phase_B, "C": phase_C, "D": phase_D,
                  "E": phase_E, "F": phase_F}


def build_program(phases, fused):
    tk_lr_prev[0] = None
    tk_router_done[0] = None
    tk_tbank[0] = tk_tbank[1] = None
    tk_psL_B[0] = None
    tk_epi_cons.clear()
    tk_epi_dbg.clear()
    nc = bass.Bass("TRN2", target_bir_lowering=False, debug=False)
    st = ExitStack()
    with st:
        t = declare_drams(nc, phases, fused)
        sems = {n: S(nc, st, n) for n in
                ["ld0", "ld1", "ld2", "ld3", "ld4", "ld5", "mm", "ev", "ve",
                 "gp", "st", "st2", "dbg0", "dbg1"]}
        for ph in phases:
            PHASE_BUILDERS[ph](nc, st, sems, t)
    return nc


FUSED = [True]
PHASE_ORDER = ["A", "B", "C", "D", "E", "F"]


def kernel(**inputs):
    w, c = host_prep(inputs)
    x = np.asarray(inputs["L0_attn_x"], dtype=np.float32)
    base = {}
    base.update({k: w[k] for k in WEIGHT_SHAPES})
    base.update({k: c[k] for k in CONST_SHAPES})

    if FUSED[0]:
        nc = build_program(PHASE_ORDER, fused=True)
        needed = set()
        for phn in PHASE_ORDER:
            needed |= set(PHASE_IN[phn])
        in_maps = []
        for bcore in range(NC):
            m = {k: v for k, v in base.items() if k in needed}
            m["x"] = np.ascontiguousarray(x[bcore])
            in_maps.append(m)
        res = run_bass_kernel_spmd(nc, in_maps, core_ids=list(range(NC)))
        return np.stack([np.asarray(r["out"], dtype=np.float32) for r in res.results])

    state = [{"x": np.ascontiguousarray(x[bcore])} for bcore in range(NC)]
    for phn in PHASE_ORDER:
        nc = build_program([phn], fused=False)
        in_maps = []
        for bcore in range(NC):
            m = {}
            for k in PHASE_IN[phn]:
                if k in base:
                    m[k] = base[k]
                else:
                    m[k] = state[bcore][k]
            in_maps.append(m)
        res = run_bass_kernel_spmd(nc, in_maps, core_ids=list(range(NC)))
        for bcore in range(NC):
            for k in PHASE_OUT[phn]:
                state[bcore][k] = np.asarray(res.results[bcore][k])
    return np.stack([state[bcore]["out"].astype(np.float32) for bcore in range(NC)])


# version 12
# speedup vs baseline: 1.0713x; 1.0713x over previous
import math
from contextlib import ExitStack

import numpy as np
import ml_dtypes

import concourse.bass as bass
import concourse.mybir as mybir
from concourse.bass_utils import run_bass_kernel_spmd

F32 = mybir.dt.float32
BF16 = mybir.dt.bfloat16
AF = mybir.ActivationFunctionType
ALU = mybir.AluOpType
AX = mybir.AxisListType

D = 1024
T = 4096
NC = 8
NG = 8
GT = 512
NCH = D // 128
E0, U0 = 8, 4096
E1, U1 = 8, 512
DFF = 4096
H, HD, KVH = 16, 64, 4
WINDOW = 256
EPS = 1e-5


class S:

    def __init__(self, nc, st, name):
        self.h = st.enter_context(nc.semaphore(name))
        self.v = 0

    def inc(self, inst, n=1):
        inst.then_inc(self.h, n)
        self.v += n
        return (self.h, self.v)

    def tok(self):
        return (self.h, self.v)


def wt(eng, token):
    if token is not None:
        eng.wait_ge(token[0], token[1])


def _bf(x):
    return np.ascontiguousarray(np.asarray(x, np.float32).astype(ml_dtypes.bfloat16))


def _f32c(x):
    return np.ascontiguousarray(np.asarray(x, dtype=np.float32))


def host_prep(inputs):
    d = {k: np.asarray(v) for k, v in inputs.items()}
    w = {}
    w["wqT0"] = _f32c(d["L0_attn_q_proj_weight"].T)
    w["wkT0"] = _f32c(d["L0_attn_k_proj_weight"].T)
    w["wvT0"] = _f32c(d["L0_attn_v_proj_weight"].T)
    w["woT0"] = _f32c(d["L0_attn_o_proj_weight"].T)
    w["wr0T"] = _f32c(d["L0_ffn_w_router"].T)
    w["wupT"] = _f32c(d["L0_ffn_w_up"].transpose(0, 2, 1))
    w["bup"] = _f32c(d["L0_ffn_b_up"].reshape(E0, 32, 128).transpose(2, 0, 1))
    w["wdownT"] = _f32c(d["L0_ffn_w_down"].transpose(0, 2, 1))
    w["bdown"] = _f32c(d["L0_ffn_b_down"])
    for k in ["Wq", "Wk", "Wv", "Wo", "Wf1", "Wf2"]:
        w[k.lower() + "1"] = _f32c(d[f"L1_attn_{k}"])
    for k in ["bq", "bk", "bv", "bo", "bf1", "bf2", "g1", "b1", "g2", "b2"]:
        w[k + "1"] = _f32c(d[f"L1_attn_{k}"]).reshape(1, -1)
    qord = []
    for cch in range(8):
        qord += [cch, cch + 4] if cch < 4 else [cch + 4, cch + 8]
    perm_cols = np.concatenate([np.arange(h * 64, (h + 1) * 64) for h in qord])
    w["wq1"] = _f32c(w["wq1"][:, perm_cols])
    w["bq1"] = _f32c(w["bq1"][:, perm_cols])
    w["wo1"] = _f32c(w["wo1"][perm_cols, :])
    w["wr1T"] = _f32c(d["L1_ffn_w_router"].T)
    w["wfcT"] = _f32c(d["L1_ffn_w_fc"].transpose(0, 2, 1))
    w["wprojT"] = _f32c(d["L1_ffn_w_proj"].transpose(0, 2, 1))

    c = {}
    c["ident"] = _bf(np.eye(128, dtype=np.float32))
    c["ones1"] = _bf(np.ones((1, 512), dtype=np.float32))
    c["ones1f"] = _f32c(np.ones((1, 512), dtype=np.float32))
    kk = np.arange(128)[:, None]
    qq = np.arange(512)[None, :]
    c["masks_c"] = _bf(np.stack([(kk <= qq - 128 * dl) for dl in range(4)]))
    m2 = np.stack([(kk <= qq - 128 * dl) for dl in range(4)])
    c["masks_c2"] = _bf(np.concatenate([m2, m2], axis=2))
    c["masks_cb"] = _bf(np.where(np.concatenate([m2, m2], axis=2), 0.0, -1e9))
    mw = np.stack([((128 * r + qq - kk >= 0) & (128 * r + qq - kk < WINDOW))
                   for r in range(-3, 3)])
    c["masks_wb"] = _bf(np.where(mw, 0.0, -1e9))
    c["masks_w"] = _bf(np.stack([((128 * r + qq - kk >= 0) & (128 * r + qq - kk < WINDOW))
                                 for r in range(-3, 3)]))
    half = HD // 2
    theta = (1.0 / 10000.0 ** (np.arange(half, dtype=np.float64) / half))
    fr = np.outer(theta, np.arange(T, dtype=np.float64))
    cosr = np.cos(fr).astype(np.float32)
    sinr = np.sin(fr).astype(np.float32)
    c["ropeC"] = _f32c(np.concatenate([cosr, cosr, cosr, cosr], 0))
    c["ropeS"] = _f32c(np.concatenate([-sinr, sinr, -sinr, sinr], 0))
    perm = np.zeros((128, 128), np.float32)
    for r in range(128):
        base = (r // 64) * 64
        perm[(r - base + 32) % 64 + base, r] = 1.0
    c["permR"] = _bf(perm)
    c["identf"] = _f32c(np.eye(128, dtype=np.float32))
    return w, c


WEIGHT_SHAPES = {
    "wqT0": [D, D], "wkT0": [D, D], "wvT0": [D, D], "woT0": [D, D],
    "wr0T": [D, 8], "wupT": [E0, D, U0], "bup": [128, E0, 32],
    "wdownT": [E0, U0, D], "bdown": [E0, D],
    "wq1": [D, D], "wk1": [D, 256], "wv1": [D, 256], "wo1": [D, D],
    "wf11": [D, DFF], "wf21": [DFF, D],
    "bq1": [1, D], "bk1": [1, 256], "bv1": [1, 256], "bo1": [1, D],
    "bf11": [1, DFF], "bf21": [1, D],
    "g11": [1, D], "b11": [1, D], "g21": [1, D], "b21": [1, D],
    "wr1T": [D, 8], "wfcT": [E1, D, U1], "wprojT": [E1, U1, D],
}
CONST_SHAPES = {
    "ident": ([128, 128], BF16), "ones1": ([1, 512], BF16), "ones1f": ([1, 512], F32),
    "masks_c": ([4, 128, 512], BF16), "masks_c2": ([4, 128, 1024], BF16),
    "masks_cb": ([4, 128, 1024], BF16),
    "masks_w": ([6, 128, 512], BF16), "masks_wb": ([6, 128, 512], BF16),
    "ropeC": ([128, T], F32), "ropeS": ([128, T], F32), "permR": ([128, 128], BF16),
    "identf": ([128, 128], F32),
}
INTER_SHAPES = {
    "xnT": ([128, NCH * T], BF16),
    "w0": ([128, 32 * 8], F32),
    "xa": ([T, D], F32),
    "x1": ([T, D], F32),
    "xn1T": ([128, NCH * T], BF16),
    "m1": ([128, 32 * 8], F32),
    "xi": ([T, D], F32),
    "xab": ([T, D], F32),
}
PHASE_IN = {
    "A": ["x", "wr0T", "ident"],
    "B": ["x", "xnT", "wqT0", "wkT0", "wvT0", "woT0", "masks_c2", "ones1f"],
    "C": ["xa", "xnT", "w0", "wupT", "bup", "wdownT", "bdown", "ones1"],
    "D": ["x1", "wq1", "wk1", "wv1", "wo1", "bq1", "bk1", "bv1", "bo1",
          "g11", "b11", "wr1T", "ident", "masks_wb", "ropeC", "ropeS", "ones1",
          "ones1f", "permR", "identf"],
    "E": ["xi", "x1", "wf11", "wf21", "bf11", "bf21", "g21", "b21", "identf", "ones1"],
    "F": ["xab", "xn1T", "m1", "wfcT", "wprojT"],
}
PHASE_OUT = {
    "A": ["xnT", "w0"], "B": ["xa"], "C": ["x1"],
    "D": ["xi", "xn1T", "m1"], "E": ["xab"], "F": ["out"],
}
DEBUG_B = [False]


def declare_drams(nc, phases, fused):
    t = {}
    needed = set()
    for ph in phases:
        needed |= set(PHASE_IN[ph])
    if "x" in needed:
        t["x"] = nc.dram_tensor("x", [T, D], F32, kind="ExternalInput").ap()
    for n, shp in WEIGHT_SHAPES.items():
        if n in needed:
            t[n] = nc.dram_tensor(n, shp, F32, kind="ExternalInput").ap()
    for n, (shp, dt) in CONST_SHAPES.items():
        if n in needed:
            t[n] = nc.dram_tensor(n, shp, dt, kind="ExternalInput").ap()
    produced = set()
    for ph in phases:
        produced |= set(PHASE_OUT[ph])
    for n, (shp, dt) in INTER_SHAPES.items():
        if n in produced and n in needed:
            kind = "Internal" if fused else "ExternalOutput"
        elif n in produced:
            kind = "Internal" if fused else "ExternalOutput"
        elif n in needed:
            kind = "ExternalInput"
        else:
            continue
        t[n] = nc.dram_tensor(n, shp, dt, kind=kind).ap()
    if "E" in phases:
        t["dscr"] = nc.dram_tensor("dscr", [1, DFF], F32, kind="Internal").ap()
    if "F" in phases:
        t["out"] = nc.dram_tensor("out", [T, D], F32, kind="ExternalOutput").ap()
    if DEBUG_B[0]:
        t["dbg_ao"] = nc.dram_tensor("dbg_ao", [128, 1024], F32, kind="ExternalOutput").ap()
        t["dbg_pt"] = nc.dram_tensor("dbg_pt", [128, 2 * 1024], BF16, kind="ExternalOutput").ap()
        t["dbg_qt"] = nc.dram_tensor("dbg_qt", [128, 512], BF16, kind="ExternalOutput").ap()
        t["dbg_kt"] = nc.dram_tensor("dbg_kt", [128, 512], BF16, kind="ExternalOutput").ap()
        t["dbg_vp"] = nc.dram_tensor("dbg_vp", [128, 4 * 130], BF16, kind="ExternalOutput").ap()
    return t


def _xview(dram_ap, g, gt=GT):
    return dram_ap[g * gt:(g + 1) * gt, :].rearrange("(s p) d -> p s d", p=128)


def _xnT_view(dram_ap, g):
    return dram_ap.rearrange("p (c t) -> p c t", c=NCH)[:, :, g * GT:(g + 1) * GT]



def phase_A(nc, st, sems, t):
    sW, sMM, sEV, sVE, sGP = (sems[k] for k in ["ld0", "mm", "ev", "ve", "gp"])
    sX = [sems["ld1"], sems["ld2"]]
    sI = sems["ld3"]
    sSTp = [sems["st"], sems["st2"]]
    sW0 = sems["dbg1"]

    with ExitStack() as ph, nc.Block("phA") as blk:
        def sb(name, shape, dt):
            return ph.enter_context(nc.sbuf_tensor(name, shape, dt))

        xg = [sb(f"A_xg{i}", [128, 4 * 1024], F32) for i in range(2)]
        sq = [sb(f"A_sq{i}", [128, 1024], F32) for i in range(2)]
        ssum = sb("A_ssum", [128, 8], F32)
        negm = sb("A_negm", [128, 8], F32)
        ssq = sb("A_ssq", [128, 8], F32)
        lnv = sb("A_lnv", [128, 8], F32)
        rstd = sb("A_rstd", [128, 8], F32)
        mrstd = sb("A_mrstd", [128, 8], F32)
        xn = sb("A_xn", [128, 4 * 1024], BF16)
        xnT = [sb(f"A_xnT{i}", [128, 8 * 512], BF16) for i in range(2)]
        wr = sb("A_wr", [128, 8 * 8], BF16)
        lrt = sb("A_lrt", [128, 4 * 8], F32)
        m8 = sb("A_m8", [128, 4 * 8], F32)
        nm1 = sb("A_nm1", [128, 8], F32)
        m2c = sb("A_m2c", [128, 8], F32)
        msk = sb("A_msk", [128, 4 * 8], F32)
        p8 = sb("A_p8", [128, 4 * 8], F32)
        ws = sb("A_ws", [128, 8], F32)
        wiv = sb("A_wiv", [128, 8], F32)
        wfin = sb("A_wfin", [128, 32 * 8], F32)
        idt = sb("A_idt", [128, 128], BF16)
        epsb = sb("A_epsb", [128, 8], F32)
        psT = [ph.enter_context(nc.psum_tensor(f"A_psT{i}", [128, 512], BF16)) for i in range(2)]
        psR = ph.enter_context(nc.psum_tensor("A_psR", [128, 4 * 8], F32))

        tk_const = [None]

        tk_ident = [None]

        def _(sy):
            tk_ident[0] = sI.inc(sy.dma_start(out=idt[:], in_=t["ident"]), 16)
        blk.sync(_)

        def _(gp):
            tk_const[0] = sW.inc(
                gp.dma_start(out=wr.ap().rearrange("p (c e) -> p c e", c=8),
                             in_=t["wr0T"].rearrange("(c p) e -> p c e", p=128)), 16)
        blk.gpsimd(_)

        tk_eps = [None]

        def _(ve):
            tk_eps[0] = sVE.inc(ve.memset(epsb[:], EPS))
        blk.vector(_)

        tk_x = [None, None]
        tk_xn_done = [None] * NG
        tk_xnT_store = [None, None]

        for g in range(NG):
            b = g % 2

            def _(sy, g=g, b=b):
                if g >= 2:
                    wt(sy, tk_xn_done[g - 2])
                tk_x[b] = sX[b].inc(
                    sy.dma_start(out=xg[b].ap().rearrange("p (s d) -> p s d", s=4),
                                 in_=_xview(t["x"], g)), 16)
            blk.sync(_)

            tk_ln = [None] * 4
            for u in range(4):
                cs, ce = u * 1024, (u + 1) * 1024
                col = slice(2 * u, 2 * u + 1)

                def _(ve, b=b, cs=cs, ce=ce, col=col, u=u):
                    if u == 0:
                        wt(ve, tk_x[b])
                    ve.tensor_reduce(ssum[:, col], xg[b][:, cs:ce], AX.X, ALU.add)
                    ve.drain()
                    sVE.inc(ve.tensor_scalar(negm[:, col], ssum[:, col], -1.0 / D, None, ALU.mult))
                blk.vector(_)
                tvv = sVE.tok()

                def _(s, b=b, cs=cs, ce=ce, col=col, tvv=tvv):
                    wt(s, tvv)
                    s.activation(sq[b][:], xg[b][:, cs:ce], AF.Square,
                                 bias=negm[:, col], accum_out=ssq[:, col])
                    s.drain()
                    if tk_eps[0] is not None:
                        wt(s, tk_eps[0]); tk_eps[0] = None
                    s.activation(lnv[:, col], ssq[:, col], AF.Ln, bias=epsb[:, 0:1], scale=1.0 / D)
                    s.drain()
                    sEV.inc(s.activation(rstd[:, col], lnv[:, col], AF.Exp, scale=-0.5))
                blk.scalar(_)
                tss = sEV.tok()

                def _(ve, col=col, tss=tss):
                    wt(ve, tss)
                    ve.drain()
                    sVE.inc(ve.tensor_mul(mrstd[:, col], negm[:, col], rstd[:, col]))
                blk.vector(_)
                tmr = sVE.tok()

                def _(s, b=b, cs=cs, ce=ce, col=col, tmr=tmr):
                    wt(s, tmr)
                    sEV.inc(s.activation(xn[:, cs:ce], xg[b][:, cs:ce], AF.Identity,
                                         bias=mrstd[:, col], scale=rstd[:, col]))
                blk.scalar(_)
                tk_ln[u] = sEV.tok()
            tk_xn_done[g] = tk_ln[3]

            tk_tp = {}
            tk_cp = {}
            for u in range(4):
                for hf in range(2):
                    def _(tn, u=u, hf=hf):
                        wt(tn, tk_ln[u])
                        wt(tn, tk_ident[0])
                        wt(tn, tk_tbank[hf])
                        i = None
                        for cc in range(4):
                            ch = hf * 4 + cc
                            i = tn.transpose(psT[hf][:, cc * 128:(cc + 1) * 128],
                                             xn[:, u * 1024 + ch * 128: u * 1024 + (ch + 1) * 128],
                                             idt[:])
                        tk_tp[(u, hf)] = sMM.inc(i)
                    blk.tensor(_)

                    def _(ve, b=b, u=u, hf=hf):
                        wt(ve, tk_tp[(u, hf)])
                        if u == 0 and hf == 0:
                            wt(ve, tk_xnT_store[b])
                        dst = xnT[b].ap().rearrange("p (c q) -> p c q", c=8)[
                            :, hf * 4:(hf + 1) * 4, u * 128:(u + 1) * 128]
                        tk_cp[(u, hf)] = sVE.inc(
                            ve.tensor_copy(dst, psT[hf].ap().rearrange("p (c q) -> p c q", c=4)))
                        tk_tbank[hf] = tk_cp[(u, hf)]
                    blk.vector(_)

            tk_xnT_all = tk_cp[(3, 1)]

            tk_lr = [None] * 4
            for u in range(4):
                def _(tn, b=b, u=u):
                    wt(tn, tk_cp[(u, 0)])
                    wt(tn, tk_cp[(u, 1)])
                    if g == 0 and u == 0:
                        wt(tn, tk_const[0])
                    wt(tn, tk_lr_prev[0])
                    i = None
                    for cch in range(8):
                        i = tn.matmul(psR[:, u * 8:(u + 1) * 8],
                                      xnT[b][:, cch * 512 + u * 128: cch * 512 + u * 128 + 128],
                                      wr[:, cch * 8:(cch + 1) * 8],
                                      start=(cch == 0), stop=(cch == 7))
                    tk_lr[u] = sMM.inc(i)
                blk.tensor(_)

            def _(s, tk=list(tk_lr)):
                wt(s, tk[3])
                sEV.inc(s.activation(lrt[:], psR[:], AF.Copy))
            blk.scalar(_)
            tk_lrt = sEV.tok()
            tk_lr_prev[0] = tk_lrt

            def _(ve, tk_lrt=tk_lrt):
                wt(ve, tk_lrt)
                wt(ve, tk_router_done[0])
                i = None
                for u in range(4):
                    i = ve.max(out=m8[:, u * 8:(u + 1) * 8], in_=lrt[:, u * 8:(u + 1) * 8])
                sVE.inc(i)
            blk.vector(_)
            tk_m8 = sVE.tok()

            def _(gp, tk_m8=tk_m8):
                wt(gp, tk_m8)
                i = None
                for u in range(4):
                    gp.tensor_copy(m2c[:, 2 * u:2 * u + 1], m8[:, u * 8 + 1:u * 8 + 2])
                    gp.tensor_scalar(nm1[:, 2 * u:2 * u + 1], m8[:, u * 8:u * 8 + 1],
                                     -1.0, None, ALU.mult)
                    gp.drain()
                    i = gp.tensor_scalar(msk[:, u * 8:(u + 1) * 8], lrt[:, u * 8:(u + 1) * 8],
                                         m2c[:, 2 * u:2 * u + 1], None, ALU.is_ge)
                sGP.inc(i)
            blk.gpsimd(_)
            tk_msk = sGP.tok()

            def _(s, tk_msk=tk_msk):
                wt(s, tk_msk)
                i = None
                for u in range(4):
                    i = s.activation(p8[:, u * 8:(u + 1) * 8], lrt[:, u * 8:(u + 1) * 8],
                                     AF.Exp, bias=nm1[:, 2 * u:2 * u + 1])
                sEV.inc(i)
            blk.scalar(_)
            tk_p8 = sEV.tok()

            def _(ve, g=g, tk_p8=tk_p8):
                wt(ve, tk_p8)
                i = None
                for u in range(4):
                    sl = slice(u * 8, (u + 1) * 8)
                    ve.tensor_mul(p8[:, sl], p8[:, sl], msk[:, sl])
                    ve.drain()
                    ve.tensor_reduce(ws[:, 2 * u:2 * u + 1], p8[:, sl], AX.X, ALU.add)
                    ve.drain()
                    ve.reciprocal(wiv[:, 2 * u:2 * u + 1], ws[:, 2 * u:2 * u + 1])
                    ve.drain()
                    i = ve.tensor_scalar(wfin[:, (g * 4 + u) * 8:(g * 4 + u + 1) * 8],
                                         p8[:, sl], wiv[:, 2 * u:2 * u + 1], None, ALU.mult)
                sVE.inc(i)
            blk.vector(_)
            tk_router_done[0] = sVE.tok()

            def _(sy, g=g, b=b, tk_xnT_all=tk_xnT_all):
                wt(sy, tk_xnT_all)
                tk_xnT_store[b] = sSTp[b].inc(
                    sy.dma_start(out=_xnT_view(t["xnT"], g),
                                 in_=xnT[b].ap().rearrange("p (c q) -> p c q", c=8)), 16)
            blk.sync(_)

        def _(sy):
            wt(sy, tk_router_done[0])
            fin = sW0.inc(sy.dma_start(out=t["w0"], in_=wfin[:]), 16)
            wt(sy, fin)
            wt(sy, tk_xnT_store[0])
            wt(sy, tk_xnT_store[1])
        blk.sync(_)


tk_lr_prev = [None]
tk_router_done = [None]
tk_tbank = [None, None]
tk_psL_B = [None]
tk_epi_cons = {}
tk_epi_dbg = {}





def phase_B(nc, st, sems, t):
    sWp = [sems["ld0"], sems["ld5"]]
    sMM, sEV, sVE = (sems[k] for k in ["mm", "ev", "ve"])
    sX = [sems["ld1"], sems["ld2"]]
    sI = sems["ld3"]
    sW2 = sems["ld4"]
    sSTp = [sems["st"], sems["st2"]]

    aot_stack = ExitStack()
    AOT = [aot_stack.enter_context(nc.sbuf_tensor(f"B_aot{p}", [128, T], BF16))
           for p in range(8)]

    with ExitStack() as ph, nc.Block("phB1") as blk:
        def sb(name, shape, dt):
            return ph.enter_context(nc.sbuf_tensor(name, shape, dt))

        mks = sb("B_mks", [128, 4 * 1024], BF16)
        ones1f = sb("B_ones1f", [1, 512], F32)
        xnTg = [sb(f"B_xnTg{i}", [128, NCH * GT], BF16) for i in range(2)]
        wqp = [sb(f"B_wq{i}", [128, 8 * 128], BF16) for i in range(2)]
        wkp = [sb(f"B_wk{i}", [128, 8 * 128], BF16) for i in range(2)]
        wvp = [sb(f"B_wv{i}", [128, 8 * 128], BF16) for i in range(2)]
        QT = [sb(f"B_qt{i}", [128, T], BF16) for i in range(2)]
        KT = [sb(f"B_kt{i}", [128, T], BF16) for i in range(2)]
        Vp = [sb(f"B_vp{i}", [128, 32 * 130], BF16) for i in range(2)]
        PT = [sb(f"B_pt{r}", [128, 1024], BF16) for r in range(4)]
        ao = sb("B_ao", [128, 2 * 512], F32)
        linv = sb("B_linv", [1, 2 * 512], F32)
        lnl = sb("B_lnl", [1, 2 * 512], F32)

        psP = [ph.enter_context(nc.psum_tensor(f"B_psP{i}", [128, 512], F32)) for i in range(2)]
        psS = [ph.enter_context(nc.psum_tensor(f"B_psS{r}", [128, 1024], F32)) for r in range(2)]
        psV = [ph.enter_context(nc.psum_tensor(f"B_psV{h}", [65, 512], F32)) for h in range(2)]
        psL = psS[0]

        tk_res = [None]

        def _(sy):
            sI.inc(sy.dma_start(out=mks.ap().rearrange("p (m q) -> p m q", m=4),
                                in_=t["masks_c2"].rearrange("m p q -> p m q")), 16)
            tk_res[0] = sI.inc(sy.dma_start(out=ones1f[:], in_=t["ones1f"]), 16)
        blk.sync(_)

        def _(ve):
            i = None
            for bb in range(2):
                i = ve.memset(Vp[bb].ap().rearrange("p (k h c) -> p k h c", h=2, c=65)
                              [:, :, :, 64:65], 1.0)
            sVE.inc(i)
        blk.vector(_)
        tk_vones = sVE.tok()

        tk_w = [None, None]
        tk_projdone = [None, None]
        tk_flashdone = [None, None]
        tk_evP = {}
        tk_exp = {}
        tk_pvdone = {}
        tk_epi = {}
        tk_gproj = {}
        tk_xng = [None, None]

        for p in range(8):
            b = p % 2

            def _(gp, p=p, b=b):
                if p >= 2:
                    wt(gp, tk_projdone[b])
                for nm, buf in [("wqT0", wqp[b]), ("wkT0", wkp[b]), ("wvT0", wvp[b])]:
                    tk_w[b] = sWp[b].inc(gp.dma_start(
                        out=buf.ap().rearrange("p (c m) -> p c m", c=8),
                        in_=t[nm][:, p * 128:(p + 1) * 128].rearrange("(c pp) m -> pp c m",
                                                                      pp=128)), 16)
            blk.gpsimd(_)

            last_ev = None
            for g in range(NG):
                gb = g % 2

                def _(sy, p=p, g=g, gb=gb):
                    wt(sy, tk_gproj.get(gb))
                    tk_xng[gb] = sX[gb].inc(sy.dma_start(
                        out=xnTg[gb].ap().rearrange("p (c q) -> p c q", c=8),
                        in_=_xnT_view(t["xnT"], g)), 16)
                blk.sync(_)

                pmb = sMM.v
                peb = sEV.v

                def ev_tok(idx):
                    return (sEV.h, peb + idx + 1)

                def _(tn, g=g, gb=gb, p=p, b=b, pmb=pmb):
                    if g == 0:
                        wt(tn, tk_res[0])
                        wt(tn, tk_w[b])
                        if p >= 2:
                            wt(tn, tk_flashdone[b])
                    wt(tn, tk_xng[gb])
                    for idx in range(6):
                        bank = (g * 6 + idx) % 2
                        wt(tn, tk_evP.get(bank))
                        tk_evP[bank] = ev_tok(idx)
                        i = None
                        if idx == 0:
                            for c in range(NCH):
                                i = tn.matmul(psP[bank][:], wqp[b][:, c * 128:(c + 1) * 128],
                                              xnTg[gb][:, c * GT:(c + 1) * GT],
                                              start=(c == 0), stop=(c == NCH - 1))
                        elif idx == 1:
                            for c in range(NCH):
                                i = tn.matmul(psP[bank][:], wkp[b][:, c * 128:(c + 1) * 128],
                                              xnTg[gb][:, c * GT:(c + 1) * GT],
                                              start=(c == 0), stop=(c == NCH - 1))
                        else:
                            ts = idx - 2
                            for c in range(NCH):
                                i = tn.matmul(psP[bank][:, 0:128],
                                              xnTg[gb][:, c * GT + ts * 128: c * GT + (ts + 1) * 128],
                                              wvp[b][:, c * 128:(c + 1) * 128],
                                              start=(c == 0), stop=(c == NCH - 1))
                        sMM.inc(i)
                blk.tensor(_)

                def _(sc, g=g, b=b, pmb=pmb):
                    for idx in range(6):
                        bank = (g * 6 + idx) % 2
                        wt(sc, (sMM.h, pmb + idx + 1))
                        if idx == 0:
                            wt(sc, tk_vones)
                        if idx == 0:
                            i = sc.activation(QT[b][:, g * GT:(g + 1) * GT], psP[bank][:], AF.Copy)
                        elif idx == 1:
                            i = sc.activation(KT[b][:, g * GT:(g + 1) * GT], psP[bank][:], AF.Copy)
                        else:
                            kt = g * 4 + (idx - 2)
                            dst = Vp[b].ap().rearrange("p (k h c) -> p k h c", h=2, c=65)[
                                :, kt, :, 0:64]
                            i = sc.activation(dst, psP[bank][:, 0:128].rearrange(
                                "p (h c) -> p h c", h=2), AF.Copy)
                        last = sEV.inc(i)
                blk.scalar(_)
                last_ev = (sEV.h, sEV.v)
                tk_gproj[gb] = last_ev
            tk_projdone[b] = last_ev

            for g in range(NG):
                nkt = 4 * g + 4
                mb = sMM.v
                eb = sEV.v
                vb = sVE.v
                _mb = [mb]
                _nkt = [nkt]

                def S_tok(kt, mb=None, nkt=None):
                    return (sMM.h, _mb[0] + (1 if kt == 0 else 2 * kt))

                def PV_tok(kt):
                    return (sMM.h, _mb[0] + (2 * kt + 3 if kt < _nkt[0] - 1 else 2 * _nkt[0]))

                def exp_tok(kt):
                    return (sEV.h, eb + kt + 1)

                def mask_tok(kt, g=g):
                    dl = kt - 4 * g
                    return (sVE.h, vb + dl + 1)

                def pt_tok(kt, g=g):
                    return mask_tok(kt) if kt - 4 * g >= 0 else exp_tok(kt)

                def _(tn, g=g, nkt=nkt, b=b, mb=mb):
                    if g == 0:
                        wt(tn, tk_projdone[b])
                    for kt in range(nkt):
                        rot4 = kt % 4
                        if kt >= 2:
                            wt(tn, exp_tok(kt - 2))
                        if kt < 2 and tk_psL_B[0] is not None:
                            wt(tn, tk_psL_B[0])
                        i = None
                        for h in range(2):
                            pb = h * 64
                            i = tn.matmul(psS[kt % 2][:, h * 512:(h + 1) * 512],
                                          KT[b][pb:pb + 64, kt * 128:(kt + 1) * 128],
                                          QT[b][pb:pb + 64, g * GT:(g + 1) * GT],
                                          start=True, stop=True)
                        sMM.inc(i)
                        if kt >= 1:
                            ktp = kt - 1
                            wt(tn, pt_tok(ktp))
                            if ktp == 0 and tk_psL_B[0] is not None:
                                wt(tn, tk_psL_B[0])
                            i = None
                            for h in range(2):
                                i = tn.matmul(psV[h][:],
                                              Vp[b][:, ktp * 130 + h * 65: ktp * 130 + h * 65 + 65],
                                              PT[ktp % 4][:, h * 512:(h + 1) * 512],
                                              start=(ktp == 0), stop=False)
                            sMM.inc(i)
                    ktp = nkt - 1
                    wt(tn, pt_tok(ktp))
                    if ktp == 0 and tk_psL_B[0] is not None:
                        wt(tn, tk_psL_B[0])
                    i = None
                    for h in range(2):
                        i = tn.matmul(psV[h][:],
                                      Vp[b][:, ktp * 130 + h * 65: ktp * 130 + h * 65 + 65],
                                      PT[ktp % 4][:, h * 512:(h + 1) * 512],
                                      start=(ktp == 0), stop=True)
                    sMM.inc(i)
                blk.tensor(_)
                assert sMM.v == mb + 2 * nkt

                def _(sc, g=g, nkt=nkt, mb=mb):
                    for kt in range(nkt):
                        wt(sc, S_tok(kt))
                        if kt >= 4:
                            wt(sc, PV_tok(kt - 4))
                        sEV.inc(sc.activation(PT[kt % 4][:], psS[kt % 2][:],
                                              AF.Exp, scale=0.125))
                blk.scalar(_)
                assert sEV.v == eb + nkt

                def _(ve, g=g, nkt=nkt):
                    for kt in range(4 * g, nkt):
                        dl = kt - 4 * g
                        wt(ve, exp_tok(kt))
                        sVE.inc(ve.tensor_mul(PT[kt % 4][:], PT[kt % 4][:],
                                              mks[:, dl * 1024:(dl + 1) * 1024]))
                blk.vector(_)
                assert sVE.v == vb + 4

                tpv = PV_tok(nkt - 1)

                def _(sc, tpv=tpv):
                    wt(sc, tpv)
                    wt(sc, tk_epi_cons.get(0))
                    wt(sc, tk_epi_cons.get(1))
                    tk_epi[0] = sEV.inc(
                        sc.activation(ao[0:65, 0:512], psV[0][:], AF.Copy))
                    tk_epi[1] = sEV.inc(
                        sc.activation(ao[0:65, 512:1024], psV[1][:], AF.Copy))
                    sc.drain()
                    sc.activation(lnl[:, 0:512], ao[64:65, 0:512], AF.Ln)
                    sc.activation(lnl[:, 512:1024], ao[64:65, 512:1024], AF.Ln)
                    sc.drain()
                    sc.activation(linv[:, 0:512], lnl[:, 0:512], AF.Exp, scale=-1.0)
                    sEV.inc(sc.activation(linv[:, 512:1024], lnl[:, 512:1024],
                                          AF.Exp, scale=-1.0))
                blk.scalar(_)
                trec = sEV.tok()

                def _(tn, trec=trec, g=g):
                    wt(tn, trec)
                    tn.matmul(psV[0][0:64, 0:512], ones1f[:, 0:64],
                              linv[:, 0:512], start=True, stop=True)
                    sMM.inc(tn.matmul(psV[1][0:64, 0:512], ones1f[:, 0:64],
                                      linv[:, 512:1024], start=True, stop=True))
                blk.tensor(_)
                tbl = sMM.tok()

                def _(ve, g=g, p=p, tbl=tbl):
                    wt(ve, tbl)
                    ve.tensor_mul(AOT[p][0:64, g * GT:(g + 1) * GT],
                                  ao[0:64, 0:512], psV[0][0:64, 0:512])
                    tk_psL_B[0] = sVE.inc(ve.tensor_mul(
                        AOT[p][64:128, g * GT:(g + 1) * GT],
                        ao[0:64, 512:1024], psV[1][0:64, 0:512]))
                    tk_epi_cons[0] = tk_psL_B[0]
                    tk_epi_cons[1] = tk_psL_B[0]
                blk.vector(_)
            tk_flashdone[b] = tk_psL_B[0]

    with ExitStack() as ph, nc.Block("phB2") as blk2:
        def sb(name, shape, dt):
            return ph.enter_context(nc.sbuf_tensor(name, shape, dt))

        woT = sb("B_woT", [128, 8 * 1024], BF16)
        xg = [sb(f"B_xg{i}", [128, 4 * 1024], F32) for i in range(2)]
        xa = [sb(f"B_xa{i}", [128, 4 * 1024], F32) for i in range(2)]
        psO = [ph.enter_context(nc.psum_tensor(f"B_psO{i}", [128, 512], F32)) for i in range(4)]

        tk_wo = [None]

        def _(gp):
            tk_wo[0] = sW2.inc(gp.dma_start(
                out=woT.ap().rearrange("p (c m) -> p c m", c=8),
                in_=t["woT0"].rearrange("(c pp) m -> pp c m", pp=128)), 16)
        blk2.gpsimd(_)

        tk_x = [None, None]
        tk_xa_store = [None, None]
        tk_psO = {}

        for g in range(NG):
            b = g % 2

            def _(sy, g=g, b=b):
                if g >= 2:
                    wt(sy, tk_xa_store[b])
                tk_x[b] = sX[b].inc(
                    sy.dma_start(out=xg[b].ap().rearrange("p (s d) -> p s d", s=4),
                                 in_=_xview(t["x"], g)), 16)
            blk2.sync(_)

            tk_add = {}
            for ts in range(4):
                for dh in range(2):
                    bank = (ts * 2 + dh) % 4

                    def _(tn, g=g, ts=ts, dh=dh, bank=bank):
                        if g == 0 and ts == 0 and dh == 0:
                            wt(tn, tk_wo[0])
                        wt(tn, tk_psO.get(bank))
                        i = None
                        for c in range(NCH):
                            i = tn.matmul(psO[bank][:],
                                          AOT[c][:, g * GT + ts * 128: g * GT + (ts + 1) * 128],
                                          woT[:, c * 1024 + dh * 512: c * 1024 + (dh + 1) * 512],
                                          start=(c == 0), stop=(c == NCH - 1))
                        sMM.inc(i)
                    blk2.tensor(_)
                    tmm = sMM.tok()

                    def _(ve, g=g, b=b, ts=ts, dh=dh, bank=bank, tmm=tmm):
                        wt(ve, tmm)
                        if ts == 0 and dh == 0:
                            wt(ve, tk_x[b])
                            if g >= 2:
                                wt(ve, tk_xa_store[b])
                        sl = slice(ts * 1024 + dh * 512, ts * 1024 + (dh + 1) * 512)
                        tk_psO[bank] = sVE.inc(
                            ve.tensor_add(xa[b][:, sl], xg[b][:, sl], psO[bank][:]))
                        tk_add[(ts, dh)] = tk_psO[bank]
                    blk2.vector(_)

            def _(sy, g=g, b=b):
                wt(sy, tk_add[(3, 1)])
                tk_xa_store[b] = sSTp[b].inc(
                    sy.dma_start(out=_xview(t["xa"], g),
                                 in_=xa[b].ap().rearrange("p (s d) -> p s d", s=4)), 16)
            blk2.sync(_)

        def _(sy):
            wt(sy, tk_xa_store[0])
            wt(sy, tk_xa_store[1])
        blk2.sync(_)
    aot_stack.close()



def phase_C(nc, st, sems, t):
    sWp = [sems["ld0"], sems["ld5"]]
    sMM, sEV, sVE = (sems[k] for k in ["mm", "ev", "ve"])
    sI = sems["ld3"]
    sXA = sems["ld1"]
    sST = sems["st"]

    with ExitStack() as ph, nc.Block("phC") as blk:
        def sb(name, shape, dt):
            return ph.enter_context(nc.sbuf_tensor(name, shape, dt))

        xnT = sb("C_xnT", [128, NCH * 2048], BF16)
        yac = sb("C_yac", [128, 16 * 1024], F32)
        w0 = sb("C_w0", [128, 32 * 8], F32)
        bup = sb("C_bup", [128, 8 * 32], F32)
        bdn = sb("C_bdn", [1, 8 * 1024], BF16)
        ones1 = sb("C_ones1", [1, 512], BF16)
        wup = [sb(f"C_wup{i}", [128, 8 * 1024], BF16) for i in range(2)]
        wdn = [sb(f"C_wdn{i}", [128, 8 * 1024], BF16) for i in range(2)]
        hsb = [sb(f"C_h{i}", [128, 8 * 512], BF16) for i in range(2)]

        psH = [ph.enter_context(nc.psum_tensor(f"C_psH{i}", [128, 512], F32)) for i in range(4)]
        psZ = [ph.enter_context(nc.psum_tensor(f"C_psZ{i}", [128, 512], F32)) for i in range(4)]

        tk_cst = [None]

        def _(sy):
            sI.inc(sy.dma_start(out=w0[:], in_=t["w0"]), 16)
            sI.inc(sy.dma_start(out=bup.ap().rearrange("p (e j) -> p e j", e=8),
                                in_=t["bup"]), 16)
            tk_cst[0] = sI.inc(sy.dma_start(out=ones1[:], in_=t["ones1"]), 16)
        blk.sync(_)

        tk_bdn = [None]

        def _(gp):
            tk_bdn[0] = sems["ld4"].inc(gp.dma_start(
                out=bdn.ap().rearrange("o (e d) -> o e d", e=8),
                in_=t["bdown"].rearrange("(o e) d -> o e d", o=1)), 16)
        blk.gpsimd(_)

        tk_xnT = [None]

        tk_w = [None, None]
        tk_wcons = [None, None]
        tk_yinit = [None]
        tk_ystore = [None]
        tk_h = {}
        tk_gpair = {}
        tk_hcons = {}
        tk_z = {}
        tk_gel = {}

        ji = 0
        for G in range(2):
            def _(sy, G=G):
                if G > 0:
                    wt(sy, tk_ystore[0])
                    wt(sy, tk_wcons[0])
                    wt(sy, tk_wcons[1])
                tk_yinit[0] = sXA.inc(
                    sy.dma_start(out=yac.ap().rearrange("p (u d) -> p u d", u=16),
                                 in_=_xview(t["xa"], G, gt=2048)), 16)
                tk_xnT[0] = sems["ld2"].inc(sy.dma_start(
                    out=xnT.ap().rearrange("p (c q) -> p c q", c=8),
                    in_=t["xnT"].rearrange("p (c t) -> p c t", c=NCH)
                    [:, :, G * 2048:(G + 1) * 2048]), 16)
            blk.sync(_)

            for e in range(E0):
                for sgi in range(4):
                    b = ji % 2

                    def _(gp, G=G, e=e, sgi=sgi, b=b):
                        if ji >= 2:
                            wt(gp, tk_wcons[b])
                        sWp[b].inc(gp.dma_start(
                            out=wup[b].ap().rearrange("p (c u) -> p c u", c=8),
                            in_=t["wupT"][e, :, sgi * 1024:(sgi + 1) * 1024]
                            .rearrange("(c p) u -> p c u", p=128)), 16)
                        tk_w[b] = sWp[b].inc(gp.dma_start(
                            out=wdn[b].ap().rearrange("p (uc d) -> p uc d", uc=8),
                            in_=t["wdownT"][e, sgi * 1024:(sgi + 1) * 1024, :]
                            .rearrange("(uc p) d -> p uc d", p=128)), 16)
                    blk.gpsimd(_)

                    for tpair in range(2):
                        for t5 in (tpair * 2, tpair * 2 + 1):
                            hr = t5 % 2
                            tok0 = t5 * 512
                            for uch in range(8):
                                hb = (t5 * 8 + uch) % 4

                                def _(tn, b=b, uch=uch, hb=hb, tok0=tok0, t5=t5):
                                    if uch == 0 and t5 % 2 == 0:
                                        wt(tn, tk_w[b])
                                        wt(tn, tk_xnT[0])
                                    wt(tn, tk_gel.get(hb))
                                    i = None
                                    for c in range(NCH):
                                        i = tn.matmul(
                                            psH[hb][:],
                                            wup[b][:, c * 1024 + uch * 128:
                                                   c * 1024 + (uch + 1) * 128],
                                            xnT[:, c * 2048 + tok0: c * 2048 + tok0 + 512],
                                            start=(c == 0), stop=(c == NCH - 1))
                                    sMM.inc(i)
                                blk.tensor(_)
                                tmm = sMM.tok()

                                def _(sc, e=e, sgi=sgi, uch=uch, hb=hb, hr=hr, tmm=tmm):
                                    wt(sc, tmm)
                                    wt(sc, tk_hcons.get(hr))
                                    if uch == 0 and tk_cst[0] is not None:
                                        wt(sc, tk_cst[0])
                                    tk_gel[hb] = sEV.inc(sc.activation(
                                        hsb[hr][:, uch * 512:(uch + 1) * 512], psH[hb][:],
                                        AF.Gelu, bias=bup[:, e * 32 + sgi * 8 + uch:
                                                          e * 32 + sgi * 8 + uch + 1]))
                                blk.scalar(_)
                            tk_gpair[t5 % 2] = (tk_gel[(t5 * 8 + 6) % 4], tk_gel[(t5 * 8 + 7) % 4])

                        for t5 in (tpair * 2, tpair * 2 + 1):
                            hr = t5 % 2
                            for tsub in range(4):
                                for dh in range(2):
                                    zb = (tsub * 2 + dh) % 4
                                    ti = G * 16 + t5 * 4 + tsub

                                    def _(tn, b=b, e=e, sgi=sgi, tsub=tsub, dh=dh, zb=zb,
                                          hr=hr, t5=t5):
                                        g01 = tk_gpair[t5 % 2]
                                        wt(tn, g01[0])
                                        wt(tn, g01[1])
                                        wt(tn, tk_z.get(zb))
                                        if sgi == 0 and tk_bdn[0] is not None:
                                            wt(tn, tk_bdn[0])
                                        i = None
                                        for uc in range(8):
                                            i = tn.matmul(
                                                psZ[zb][:],
                                                hsb[hr][:, uc * 512 + tsub * 128:
                                                         uc * 512 + (tsub + 1) * 128],
                                                wdn[b][:, uc * 1024 + dh * 512:
                                                       uc * 1024 + (dh + 1) * 512],
                                                start=(uc == 0), stop=(uc == 7 and sgi != 0))
                                        if sgi == 0:
                                            i = tn.matmul(
                                                psZ[zb][:], ones1[:, 0:128],
                                                bdn[:, e * 1024 + dh * 512:
                                                    e * 1024 + (dh + 1) * 512],
                                                start=False, stop=True)
                                        tk_wcons[b] = sMM.inc(i)
                                        tk_hcons[hr] = tk_wcons[b]
                                    blk.tensor(_)
                                    tmm = sMM.tok()

                                    def _(ve, e=e, tsub=tsub, dh=dh, zb=zb, ti=ti,
                                          t5=t5, tmm=tmm):
                                        wt(ve, tmm)
                                        if tk_yinit[0] is not None:
                                            wt(ve, tk_yinit[0])
                                        sl = slice((t5 * 4 + tsub) * 1024 + dh * 512,
                                                   (t5 * 4 + tsub) * 1024 + (dh + 1) * 512)
                                        tk_z[zb] = sVE.inc(ve.scalar_tensor_tensor(
                                            yac[:, sl], psZ[zb][:],
                                            w0[:, ti * 8 + e: ti * 8 + e + 1],
                                            yac[:, sl], ALU.mult, ALU.add))
                                        ve.drain()
                                    blk.vector(_)

                        if e == E0 - 1 and sgi == 3:
                            for t5 in (tpair * 2, tpair * 2 + 1):
                                def _(sy, G=G, t5=t5, tok=sVE.tok()):
                                    wt(sy, tok)
                                    tk_ystore[0] = sST.inc(sy.dma_start(
                                        out=t["x1"][G * 2048 + t5 * 512:
                                                    G * 2048 + (t5 + 1) * 512, :]
                                        .rearrange("(s p) d -> p s d", p=128),
                                        in_=yac.ap().rearrange("p (u d) -> p u d", u=16)
                                        [:, t5 * 4:(t5 + 1) * 4, :]), 16)
                                blk.sync(_)
                    ji += 1



        def _(sy):
            wt(sy, tk_ystore[0])
        blk.sync(_)




def phase_D(nc, st, sems, t):
    sMM, sEV, sVE = (sems[k] for k in ["mm", "ev", "ve"])
    sI = sems["ld3"]
    sIc = sems["ld4"]
    sX1 = sems["ld1"]
    sRP = sems["ld2"]
    sSTx = sems["st"]
    sSTn = sems["st2"]

    with ExitStack() as ph, nc.Block("phD") as blk:
        def sb(name, shape, dt):
            return ph.enter_context(nc.sbuf_tensor(name, shape, dt))

        wq = sb("D_wq", [128, 8 * 1024], BF16)
        wk = sb("D_wk", [128, 8 * 256], BF16)
        wv = sb("D_wv", [128, 8 * 256], BF16)
        wo = sb("D_wo", [128, 8 * 1024], BF16)
        wr = sb("D_wr", [128, 8 * 8], BF16)
        mks = sb("D_mks", [128, 6 * 512], BF16)
        idt = sb("D_idt", [128, 128], BF16)
        idtf = sb("D_idtf", [128, 128], F32)
        prm = sb("D_prm", [128, 128], BF16)
        ones1 = sb("D_ones1", [1, 512], BF16)
        ones1f = sb("D_ones1f", [1, 512], F32)
        g1c = sb("D_g1c", [128, 8], F32)
        b1c = sb("D_b1c", [128, 8], F32)
        b1cb = sb("D_b1cb", [128, 8], BF16)
        bqr = sb("D_bqr", [1, 1024], F32)
        bkr = sb("D_bkr", [1, 256], F32)
        bvr = sb("D_bvr", [1, 256], F32)
        bqe = sb("D_bqe", [1, 1024], BF16)
        bke = sb("D_bke", [1, 256], BF16)
        bve = sb("D_bve", [1, 256], BF16)
        boe = sb("D_boe", [1, 1024], BF16)
        epsb = sb("D_epsb", [128, 8], F32)
        KT1 = [sb(f"D_kt{i}", [128, 8 * 128], BF16) for i in range(2)]
        Vp1 = sb("D_vp", [128, 8 * 260], BF16)
        x1g = sb("D_x1g", [128, 4 * 1024], F32)
        xn1f = sb("D_xn1f", [128, 4 * 1024], F32)
        scr2 = sb("D_scr2", [128, 4 * 1024], F32)
        sq = [sb(f"D_sq{i}", [128, 1024], F32) for i in range(2)]
        ssum = sb("D_ssum", [128, 8], F32)
        negm = sb("D_negm", [128, 8], F32)
        ssq = sb("D_ssq", [128, 8], F32)
        lnv = sb("D_lnv", [128, 8], F32)
        rstd = sb("D_rstd", [128, 8], F32)
        mrstd = sb("D_mrstd", [128, 8], F32)
        xn1T = sb("D_xn1T", [128, 8 * 512], BF16)
        h1T = sb("D_h1T", [128, 8 * 512], BF16)
        ropC2 = [sb(f"D_ropC{i}", [128, 512], F32) for i in range(2)]
        ropS2 = [sb(f"D_ropS{i}", [128, 512], F32) for i in range(2)]
        scrT = sb("D_scrT", [128, 2 * 512], F32)
        QTb = sb("D_QTb", [128, 2 * 512], BF16)
        QT = sb("D_QT", [128, 8 * 512], BF16)
        AOT1 = sb("D_AOT1", [128, 8 * 512], BF16)
        PT = [sb(f"D_pt{r}", [128, 512], BF16) for r in range(4)]
        ao = sb("D_ao", [128, 2 * 512], F32)
        linv = sb("D_linv", [1, 2 * 512], F32)
        lnl = sb("D_lnl", [1, 2 * 512], F32)
        lrt = sb("D_lrt", [128, 4 * 8], F32)
        mx1 = sb("D_mx1", [128, 8], F32)
        m1f = sb("D_m1f", [128, 32 * 8], F32)
        xig = sb("D_xig", [128, 4 * 1024], F32)

        psA = [ph.enter_context(nc.psum_tensor(f"D_psA{i}", [128, 512], F32)) for i in range(2)]
        psT = [ph.enter_context(nc.psum_tensor(f"D_psT{i}", [128, 512], F32)) for i in range(2)]
        psS = [ph.enter_context(nc.psum_tensor(f"D_psS{i}", [128, 512], F32)) for i in range(2)]
        psV = [ph.enter_context(nc.psum_tensor(f"D_psV{i}", [65, 512], F32)) for i in range(2)]

        tk_cs = [None]
        tk_cg = [None]

        def _(sy):
            sI.inc(sy.dma_start(out=mks.ap().rearrange("p (m q) -> p m q", m=6),
                                in_=t["masks_wb"].rearrange("m p q -> p m q")), 16)
            sI.inc(sy.dma_start(out=idt[:], in_=t["ident"]), 16)
            sI.inc(sy.dma_start(out=idtf[:], in_=t["identf"]), 16)
            sI.inc(sy.dma_start(out=prm[:], in_=t["permR"]), 16)
            sI.inc(sy.dma_start(out=ones1[:], in_=t["ones1"]), 16)
            sI.inc(sy.dma_start(out=ones1f[:], in_=t["ones1f"]), 16)
            sI.inc(sy.dma_start(out=bqr[:], in_=t["bq1"]), 16)
            sI.inc(sy.dma_start(out=bkr[:], in_=t["bk1"]), 16)
            sI.inc(sy.dma_start(out=bvr[:], in_=t["bv1"]), 16)
            with nc.allow_non_contiguous_dma("tiny one-time 4KiB column loads"):
                sI.inc(sy.dma_start(out=g1c.ap().rearrange("p (c o) -> p c o", o=1),
                                    in_=t["g11"].rearrange("o (c p) -> p c o", p=128)), 16)
                tk_cs[0] = sI.inc(sy.dma_start(out=b1c.ap().rearrange("p (c o) -> p c o", o=1),
                                               in_=t["b11"].rearrange("o (c p) -> p c o", p=128)), 16)
        blk.sync(_)

        def _(gp):
            sIc.inc(gp.dma_start(out=wq.ap().rearrange("p (c m) -> p c m", c=8),
                                 in_=t["wq1"].rearrange("(c p) m -> p c m", p=128)), 16)
            sIc.inc(gp.dma_start(out=wk.ap().rearrange("p (c m) -> p c m", c=8),
                                 in_=t["wk1"].rearrange("(c p) m -> p c m", p=128)), 16)
            sIc.inc(gp.dma_start(out=wv.ap().rearrange("p (c m) -> p c m", c=8),
                                 in_=t["wv1"].rearrange("(c p) m -> p c m", p=128)), 16)
            sIc.inc(gp.dma_start(out=wo.ap().rearrange("p (c m) -> p c m", c=8),
                                 in_=t["wo1"].rearrange("(c p) m -> p c m", p=128)), 16)
            sIc.inc(gp.dma_start(out=wr.ap().rearrange("p (c e) -> p c e", c=8),
                                 in_=t["wr1T"].rearrange("(c p) e -> p c e", p=128)), 16)
            tk_cg[0] = sIc.inc(gp.dma_start(out=boe[:], in_=t["bo1"]), 16)
        blk.gpsimd(_)

        def _(ve):
            ve.memset(epsb[:], EPS)
            ve.memset(Vp1.ap().rearrange("p (k h c) -> p k h c", h=4, c=65)[:, :, :, 64:65], 1.0)
            wt(ve, tk_cs[0])
            wt(ve, tk_cg[0])
            sVE.inc(ve.tensor_copy(b1cb[:], b1c[:]))
        blk.vector(_)
        tk_pre_v = sVE.tok()

        tk_bias = [None]
        for nm, wbuf, br, be, n_out in [("q", wq, bqr, bqe, 1024), ("k", wk, bkr, bke, 256),
                                        ("v", wv, bvr, bve, 256)]:
            nhalf = (n_out + 511) // 512
            for hf in range(nhalf):
                n0, n1 = hf * 512, min((hf + 1) * 512, n_out)

                def _(tn, wbuf=wbuf, n0=n0, n1=n1, n_out=n_out):
                    wt(tn, tk_pre_v)
                    wt(tn, tk_bias[0])
                    i = None
                    for c in range(NCH):
                        i = tn.matmul(psA[0][:1, 0:n1 - n0], b1cb[:, c:c + 1],
                                      wbuf[:, c * n_out + n0: c * n_out + n1],
                                      start=(c == 0), stop=(c == NCH - 1))
                    sMM.inc(i)
                blk.tensor(_)
                tmm = sMM.tok()

                def _(ve, br=br, be=be, n0=n0, n1=n1, tmm=tmm):
                    wt(ve, tmm)
                    tk_bias[0] = sVE.inc(ve.tensor_add(be[:, n0:n1], br[:, n0:n1],
                                                       psA[0][0:1, 0:n1 - n0]))
                    ve.drain()
                blk.vector(_)

        def _(ve):
            wt(ve, tk_bias[0])
            i = None
            for wbuf, n_out in [(wq, 1024), (wk, 256), (wv, 256)]:
                for c in range(NCH):
                    i = ve.tensor_scalar(wbuf[:, c * n_out:(c + 1) * n_out],
                                         wbuf[:, c * n_out:(c + 1) * n_out],
                                         g1c[:, c:c + 1], None, ALU.mult)
                    ve.drain()
            sVE.inc(i)
        blk.vector(_)
        tk_wsc = sVE.tok()

        tk_x1 = [None]
        tk_rp = [None]
        tk_xi_store = [None]
        tk_xn1T_store = [None]
        tk_ln = {}
        tk_tbl = {}
        tk_m1v = [None]
        tk_opev = {}
        tk_psLd = [None]
        tk_ropeuse = [None]
        tk_ropeuse2 = [None, None]
        tk_aocp = {}
        tk_ttq = {}
        tk_fl2 = [None]
        tk_fl_hist = [None, None]

        for g in range(NG):
            def _(sy, g=g):
                if g > 0:
                    wt(sy, tk_ln[0])
                tk_x1[0] = sX1.inc(
                    sy.dma_start(out=x1g.ap().rearrange("p (s d) -> p s d", s=4),
                                 in_=_xview(t["x1"], g)), 16)
                if g > 1:
                    wt(sy, tk_ropeuse2[g % 2])
                sRP.inc(sy.dma_start(out=ropC2[g % 2][:],
                                     in_=t["ropeC"][:, g * GT:(g + 1) * GT]), 16)
                tk_rp[0] = sRP.inc(
                    sy.dma_start(out=ropS2[g % 2][:],
                                 in_=t["ropeS"][:, g * GT:(g + 1) * GT]), 16)
            blk.sync(_)
            ropC = ropC2[g % 2]
            ropS = ropS2[g % 2]

            for (src, dst, pass_i) in [(x1g, xn1f, 0), (xn1f, scr2, 1)]:
                def _(ve, src=src, pass_i=pass_i):
                    if pass_i == 0:
                        wt(ve, tk_x1[0])
                    else:
                        wt(ve, tk_ln[0])
                    i = None
                    for u in range(4):
                        col = slice(2 * u, 2 * u + 1)
                        ve.tensor_reduce(ssum[:, col], src[:, u * 1024:(u + 1) * 1024],
                                         AX.X, ALU.add)
                        ve.drain()
                        i = ve.tensor_scalar(negm[:, col], ssum[:, col], -1.0 / D,
                                             None, ALU.mult)
                        ve.drain()
                    sVE.inc(i)
                blk.vector(_)
                tvv = sVE.tok()

                def _(sc, src=src, tvv=tvv):
                    wt(sc, tvv)
                    i = None
                    for u in range(4):
                        col = slice(2 * u, 2 * u + 1)
                        sc.activation(sq[u % 2][:], src[:, u * 1024:(u + 1) * 1024],
                                      AF.Square, bias=negm[:, col], accum_out=ssq[:, col])
                        sc.drain()
                        sc.activation(lnv[:, col], ssq[:, col], AF.Ln,
                                      bias=epsb[:, 0:1], scale=1.0 / D)
                        sc.drain()
                        i = sc.activation(rstd[:, col], lnv[:, col], AF.Exp, scale=-0.5)
                        sc.drain()
                    sEV.inc(i)
                blk.scalar(_)
                tss = sEV.tok()

                def _(ve, tss=tss):
                    wt(ve, tss)
                    i = None
                    for u in range(4):
                        col = slice(2 * u, 2 * u + 1)
                        i = ve.tensor_mul(mrstd[:, col], negm[:, col], rstd[:, col])
                        ve.drain()
                    sVE.inc(i)
                blk.vector(_)
                tmr = sVE.tok()

                def _(sc, src=src, dst=dst, pass_i=pass_i, g=g, tmr=tmr):
                    wt(sc, tmr)
                    if pass_i == 1 and g > 0:
                        wt(sc, tk_tbl.get(1))
                    i = None
                    for u in range(4):
                        col = slice(2 * u, 2 * u + 1)
                        i = sc.activation(dst[:, u * 1024:(u + 1) * 1024],
                                          src[:, u * 1024:(u + 1) * 1024], AF.Identity,
                                          bias=mrstd[:, col], scale=rstd[:, col])
                    sEV.inc(i)
                blk.scalar(_)
                tk_ln[pass_i] = sEV.tok()

            for (src, dstT, ti) in [(xn1f, xn1T, 0), (scr2, h1T, 1)]:
                tmb = sMM.v
                tvb = sVE.v

                def _(tn, src=src, ti=ti, g=g, tvb=tvb):
                    wt(tn, tk_ln[ti])
                    if g == 0 and ti == 0:
                        wt(tn, tk_cs[0])
                    for u in range(4):
                        for hf in range(2):
                            idx = u * 2 + hf
                            wt(tn, (sVE.h, max(0, tvb + idx - 1)))
                            i = None
                            for cc in range(4):
                                ch = hf * 4 + cc
                                i = tn.transpose(psT[hf][:, cc * 128:(cc + 1) * 128],
                                                 src[:, u * 1024 + ch * 128:
                                                     u * 1024 + (ch + 1) * 128],
                                                 idtf[:])
                            sMM.inc(i)
                blk.tensor(_)

                def _(ve, dstT=dstT, ti=ti, tmb=tmb, g=g):
                    if ti == 0 and g > 0:
                        wt(ve, tk_xn1T_store[0])
                    for u in range(4):
                        for hf in range(2):
                            idx = u * 2 + hf
                            wt(ve, (sMM.h, tmb + idx + 1))
                            dst = dstT.ap().rearrange("p (c q) -> p c q", c=8)[
                                :, hf * 4:(hf + 1) * 4, u * 128:(u + 1) * 128]
                            sVE.inc(ve.tensor_copy(dst, psT[hf].ap()
                                                   .rearrange("p (c q) -> p c q", c=4)))
                blk.vector(_)
                tk_tbl[ti] = sVE.tok()

            def _(sy, g=g):
                wt(sy, tk_tbl[0])
                tk_xn1T_store[0] = sSTn.inc(
                    sy.dma_start(out=_xnT_view(t["xn1T"], g),
                                 in_=xn1T.ap().rearrange("p (c q) -> p c q", c=8)), 16)
            blk.sync(_)

            def _(tn, g=g):
                wt(tn, tk_tbl[0])
                if g > 0:
                    wt(tn, tk_opev.get(0))
                    wt(tn, tk_opev.get(1))
                else:
                    wt(tn, tk_bias[0])
                i = None
                for u in range(4):
                    for c in range(NCH):
                        i = tn.matmul(psA[0][:, u * 8:(u + 1) * 8],
                                      xn1T[:, c * 512 + u * 128: c * 512 + u * 128 + 128],
                                      wr[:, c * 8:(c + 1) * 8],
                                      start=(c == 0), stop=(c == NCH - 1))
                sMM.inc(i)
            blk.tensor(_)
            tlg = sMM.tok()

            def _(sc, tlg=tlg):
                wt(sc, tlg)
                sEV.inc(sc.activation(lrt[:], psA[0][:, 0:32], AF.Copy))
            blk.scalar(_)
            tlr = sEV.tok()

            def _(ve, g=g, tlr=tlr):
                wt(ve, tlr)
                i = None
                for u in range(4):
                    sl = slice(u * 8, (u + 1) * 8)
                    ve.tensor_reduce(mx1[:, 0:1], lrt[:, sl], AX.X, ALU.max)
                    ve.drain()
                    i = ve.tensor_scalar(m1f[:, (g * 4 + u) * 8:(g * 4 + u + 1) * 8],
                                         lrt[:, sl], mx1[:, 0:1], None, ALU.is_ge)
                    ve.drain()
                tk_m1v[0] = sVE.inc(i)
            blk.vector(_)

            tk_fl2[0] = tk_fl_hist[g % 2]

            peb = sEV.v
            pvb = sVE.v
            tok_proj = {}
            tok_perm = {}

            def _(tn, g=g, peb=peb, pvb=pvb):
                wt(tn, tk_wsc)
                wt(tn, tk_tbl[1])
                wt(tn, tk_m1v[0])

                def emit_proj(idx):
                    bank = idx % 2
                    if idx >= 2:
                        wt(tn, (sEV.h, peb + idx - 1))
                    i = None
                    if idx < 8:
                        for c in range(NCH):
                            i = tn.matmul(psA[bank][:],
                                          wq[:, c * 1024 + idx * 128: c * 1024 + (idx + 1) * 128],
                                          h1T[:, c * 512:(c + 1) * 512],
                                          start=(c == 0), stop=False)
                        i = tn.matmul(psA[bank][:], bqe[:, idx * 128:(idx + 1) * 128],
                                      ones1[:, 0:512], start=False, stop=True)
                    elif idx < 10:
                        ch = idx - 8
                        for c in range(NCH):
                            i = tn.matmul(psA[bank][:],
                                          wk[:, c * 256 + ch * 128: c * 256 + (ch + 1) * 128],
                                          h1T[:, c * 512:(c + 1) * 512],
                                          start=(c == 0), stop=False)
                        i = tn.matmul(psA[bank][:], bke[:, ch * 128:(ch + 1) * 128],
                                      ones1[:, 0:512], start=False, stop=True)
                    else:
                        ts = idx - 10
                        for c in range(NCH):
                            i = tn.matmul(psA[bank][:, 0:256],
                                          h1T[:, c * 512 + ts * 128: c * 512 + (ts + 1) * 128],
                                          wv[:, c * 256:(c + 1) * 256],
                                          start=(c == 0), stop=False)
                        i = tn.matmul(psA[bank][:, 0:256], ones1[:, 0:128], bve[:],
                                      start=False, stop=True)
                    tok_proj[idx] = sMM.inc(i)

                def emit_perm(j):
                    rot = j % 2
                    wt(tn, (sEV.h, peb + j + 1))
                    if j >= 2:
                        wt(tn, (sVE.h, pvb + (j - 2) + 1))
                    tok_perm[j] = sMM.inc(tn.matmul(psS[rot][:], prm[:],
                                                    QTb[:, rot * 512:(rot + 1) * 512],
                                                    start=True, stop=True))

                emit_proj(0)
                for idx in range(1, 14):
                    emit_proj(idx)
                    if idx - 1 < 10:
                        emit_perm(idx - 1)
                emit_perm(9)
            blk.tensor(_)

            def _(sc, g=g, pvb=pvb):
                for idx in range(14):
                    bank = idx % 2
                    wt(sc, tok_proj[idx])
                    if idx < 10:
                        rot = idx % 2
                        if idx >= 2:
                            wt(sc, (sVE.h, pvb + (idx - 2) + 1))
                        sEV.inc(sc.activation(QTb[:, rot * 512:(rot + 1) * 512],
                                              psA[bank][:], AF.Copy))
                    else:
                        kt = g * 4 + (idx - 10)
                        if idx == 10:
                            wt(sc, tk_fl2[0])
                        sEV.inc(sc.activation(
                            Vp1.ap().rearrange("p (k h c) -> p k h c", h=4, c=65)[
                                :, kt % 8, :, 0:64],
                            psA[bank][:, 0:256].rearrange("p (h c) -> p h c", h=4),
                            AF.Copy))
            blk.scalar(_)

            def _(ve, g=g):
                wt(ve, tk_rp[0])
                for idx in range(10):
                    rot = idx % 2
                    wt(ve, tok_perm[idx])
                    if idx < 8:
                        dst = QT[:, idx * 512:(idx + 1) * 512]
                    else:
                        if idx == 8:
                            wt(ve, tk_fl2[0])
                        ch4 = (g % 2) * 4
                        dst = KT1[idx - 8][:, ch4 * 128:(ch4 + 4) * 128]
                    ve.tensor_mul(scrT[:, 0:512], QTb[:, rot * 512:(rot + 1) * 512], ropC[:])
                    ve.tensor_mul(scrT[:, 512:1024], psS[rot][:], ropS[:])
                    ve.drain()
                    ve.tensor_add(dst, scrT[:, 0:512], scrT[:, 512:1024])
                    sVE.inc(ve.drain())
            blk.vector(_)
            tk_ropeuse[0] = sVE.tok()
            tk_ropeuse2[g % 2] = sVE.tok()
            tk_projg = sVE.tok()

            fmb = sMM.v
            feb = sEV.v
            fvb = sVE.v
            qinfo = []
            jq = {}
            j0c = 0
            for qh in range(16):
                kv = qh // 4
                qch = (qh % 4) + (4 if qh >= 8 else 0)
                kts = list(range(max(0, 4 * g - 2), 4 * g + 4))
                qinfo.append((qh, kv, qch, kts, j0c))
                for idx in range(len(kts)):
                    jq[j0c + idx] = (qh, idx)
                j0c += len(kts)
            nj = j0c
            tmb_of = {}
            off = fmb
            for (qh, kv, qch, kts, j0) in qinfo:
                tmb_of[qh] = (off, len(kts))
                off += 2 * len(kts) + 1
            cp_of = {}
            for (qh, kv, qch, kts, j0) in qinfo:
                cp_of[qh] = feb + j0 + len(kts) + qh + 1

            def S_tok_D(qh, idx):
                base, nk = tmb_of[qh]
                return (sMM.h, base + (idx + 1 if idx <= 1 else 2 * idx))

            def PV_tok_D(qh, idx):
                base, nk = tmb_of[qh]
                return (sMM.h, base + (2 * idx + 3 if idx < nk - 1 else 2 * nk))

            def exp_tok_D(j):
                return (sEV.h, feb + j + jq[j][0] + 1)

            def recip_tok(qh):
                return (sVE.h, fvb + 2 * qh + 1)

            def tt_tok(qh):
                return (sVE.h, fvb + 2 * qh + 2)

            def bc_tok(qh):
                base, nk = tmb_of[qh]
                return (sMM.h, base + 2 * nk + 1)

            psld_prev = tk_psLd[0]

            def _(tn, g=g):
                wt(tn, tk_projg)
                for (qh, kv, qch, kts, j0) in qinfo:
                    nk = len(kts)
                    pb = (kv % 2) * 64

                    def emit_S2(idx, qh=qh, kv=kv, qch=qch, kts=kts, j0=j0, pb=pb):
                        j = j0 + idx
                        kt = kts[idx]
                        rot = j % 2
                        if j >= 2:
                            wt(tn, exp_tok_D(j - 2))
                        mi = 4 * g - kt + 3
                        tn.matmul(psS[rot][:],
                                  KT1[kv // 2][pb:pb + 64, (kt % 8) * 128:(kt % 8) * 128 + 128],
                                  QT[:, qch * 512:(qch + 1) * 512][pb:pb + 64, :],
                                  start=True, stop=False)
                        sMM.inc(tn.matmul(psS[rot][:], idt[:],
                                          mks[:, mi * 512:(mi + 1) * 512],
                                          start=False, stop=True))

                    def emit_PV2(idx, qh=qh, kv=kv, kts=kts, j0=j0, nk=nk):
                        j = j0 + idx
                        kt = kts[idx]
                        wt(tn, exp_tok_D(j))
                        if idx == 0:
                            if qh >= 2:
                                wt(tn, tt_tok(qh - 2))
                            else:
                                wt(tn, tk_ttq.get(qh % 2))
                        sMM.inc(tn.matmul(psV[qh % 2][:],
                                          Vp1[:, (kt % 8) * 260 + kv * 65:
                                              (kt % 8) * 260 + kv * 65 + 65],
                                          PT[j % 4][:],
                                          start=(idx == 0), stop=(idx == nk - 1)))

                    emit_S2(0)
                    for idx in range(1, nk):
                        emit_S2(idx)
                        emit_PV2(idx - 1)
                    emit_PV2(nk - 1)
                    wt(tn, (sEV.h, cp_of[qh]))
                    sMM.inc(tn.matmul(psV[qh % 2][0:64, 0:512], ones1f[:, 0:64],
                                      linv[:, (qh % 2) * 512:(qh % 2 + 1) * 512],
                                      start=True, stop=True))
            blk.tensor(_)
            assert sMM.v == off

            def _(sc, g=g):
                for (qh, kv, qch, kts, j0) in qinfo:
                    nk = len(kts)
                    for idx in range(nk):
                        j = j0 + idx
                        wt(sc, S_tok_D(qh, idx))
                        if j >= 4:
                            wt(sc, PV_tok_D(*jq[j - 4]))
                        elif psld_prev is not None:
                            wt(sc, psld_prev)
                        sEV.inc(sc.activation(PT[j % 4][:], psS[j % 2][:],
                                              AF.Exp, scale=0.125))
                    wt(sc, PV_tok_D(qh, nk - 1))
                    if qh >= 2:
                        wt(sc, tt_tok(qh - 2))
                    else:
                        wt(sc, tk_ttq.get(qh % 2))
                    sc.activation(ao[0:65, (qh % 2) * 512:(qh % 2 + 1) * 512],
                                  psV[qh % 2][:], AF.Copy)
                    sc.drain()
                    sc.activation(lnl[:, (qh % 2) * 512:(qh % 2 + 1) * 512],
                                  ao[64:65, (qh % 2) * 512:(qh % 2 + 1) * 512], AF.Ln)
                    sc.drain()
                    sEV.inc(sc.activation(linv[:, (qh % 2) * 512:(qh % 2 + 1) * 512],
                                          lnl[:, (qh % 2) * 512:(qh % 2 + 1) * 512],
                                          AF.Exp, scale=-1.0))
                    assert sEV.v == cp_of[qh]
            blk.scalar(_)

            def _(ve, g=g):
                for (qh, kv, qch, kts, j0) in qinfo:
                    qhf = (qh // 4) % 2
                    ve.nop().then_inc(sVE.h, 1)
                    sVE.v += 1
                    wt(ve, bc_tok(qh))
                    sVE.inc(ve.tensor_mul(
                        AOT1[qhf * 64:(qhf + 1) * 64, qch * 512:(qch + 1) * 512],
                        ao[0:64, (qh % 2) * 512:(qh % 2 + 1) * 512],
                        psV[qh % 2][0:64, 0:512]))
                    ve.drain()
            blk.vector(_)
            assert sVE.v == fvb + 32
            tk_psLd[0] = (sVE.h, fvb + 32)
            tk_aocp[0] = (sEV.h, cp_of[14])
            tk_aocp[1] = (sEV.h, cp_of[15])
            tk_ttq[0] = tt_tok(14)
            tk_ttq[1] = tt_tok(15)
            tk_fl_hist[g % 2] = tk_psLd[0]

            omb = sMM.v
            ovb = sVE.v

            def _(tn, g=g, ovb=ovb):
                wt(tn, tk_psLd[0])
                for idx in range(8):
                    ts, dh = idx // 2, idx % 2
                    bank = idx % 2
                    if idx >= 2:
                        wt(tn, (sVE.h, ovb + idx - 1))
                    i = None
                    for c in range(NCH):
                        i = tn.matmul(psA[bank][:],
                                      AOT1[:, c * 512 + ts * 128: c * 512 + (ts + 1) * 128],
                                      wo[:, c * 1024 + dh * 512: c * 1024 + (dh + 1) * 512],
                                      start=(c == 0), stop=False)
                    i = tn.matmul(psA[bank][:], ones1[:, 0:128],
                                  boe[:, dh * 512:(dh + 1) * 512],
                                  start=False, stop=True)
                    sMM.inc(i)
            blk.tensor(_)

            def _(ve, g=g, omb=omb):
                for idx in range(8):
                    ts, dh = idx // 2, idx % 2
                    bank = idx % 2
                    wt(ve, (sMM.h, omb + idx + 1))
                    if idx == 0 and g > 0:
                        wt(ve, tk_xi_store[0])
                    sl = slice(ts * 1024 + dh * 512, ts * 1024 + (dh + 1) * 512)
                    sVE.inc(ve.tensor_add(xig[:, sl], xn1f[:, sl], psA[bank][:]))
                    ve.drain()
                tk_opev[0] = (sVE.h, sVE.v - 1)
                tk_opev[1] = (sVE.h, sVE.v)
            blk.vector(_)
            tk_xi_done = sVE.tok()

            def _(sy, g=g, tk_xi_done=tk_xi_done):
                wt(sy, tk_xi_done)
                tk_xi_store[0] = sSTx.inc(
                    sy.dma_start(out=_xview(t["xi"], g),
                                 in_=xig.ap().rearrange("p (s d) -> p s d", s=4)), 16)
            blk.sync(_)

        def _(sy):
            wt(sy, tk_m1v[0])
            fin = sems["dbg0"].inc(sy.dma_start(out=t["m1"], in_=m1f[:]), 16)
            wt(sy, fin)
            wt(sy, tk_xi_store[0])
            wt(sy, tk_xn1T_store[0])
        blk.sync(_)



def phase_E(nc, st, sems, t):
    sMM, sEV, sVE = (sems[k] for k in ["mm", "ev", "ve"])
    sI, sIc = sems["ld3"], sems["ld4"]
    sXI = sems["ld1"]
    sX1 = sems["ld2"]
    sSTp = [sems["st"], sems["st2"]]

    with ExitStack() as ph, nc.Block("phE") as blk:
        def sb(name, shape, dt):
            return ph.enter_context(nc.sbuf_tensor(name, shape, dt))

        wf1 = sb("E_wf1", [128, 8 * 4096], BF16)
        wf2 = sb("E_wf2", [128, 32 * 1024], BF16)
        g2c = sb("E_g2c", [128, 8], F32)
        b2c = sb("E_b2c", [128, 8], F32)
        b2cb = sb("E_b2cb", [128, 8], BF16)
        bf1row = sb("E_bf1row", [1, 4096], F32)
        bf1c = sb("E_bf1c", [128, 32], F32)
        bf2e = sb("E_bf2e", [1, 1024], BF16)
        idtf = sb("E_idtf", [128, 128], F32)
        ones1 = sb("E_ones1", [1, 512], BF16)
        epsb = sb("E_epsb", [128, 8], F32)
        xig = sb("E_xig", [128, 4 * 1024], F32)
        x1t = sb("E_x1t", [128, 2 * 1024], F32)
        lnscr = [sb(f"E_lnscr{i}", [128, 1024], F32) for i in range(2)]
        sq = lnscr
        ssum = sb("E_ssum", [128, 8], F32)
        negm = sb("E_negm", [128, 8], F32)
        ssq = sb("E_ssq", [128, 8], F32)
        lnv = sb("E_lnv", [128, 8], F32)
        rstd = sb("E_rstd", [128, 8], F32)
        mrstd = sb("E_mrstd", [128, 8], F32)
        h2T = sb("E_h2T", [128, 8 * 512], BF16)
        hf = sb("E_hf", [128, 32 * 256], BF16)

        psA = [ph.enter_context(nc.psum_tensor(f"E_psA{i}", [128, 512], F32)) for i in range(2)]
        psT = [ph.enter_context(nc.psum_tensor(f"E_psT{i}", [128, 512], F32)) for i in range(2)]
        psZ = [ph.enter_context(nc.psum_tensor(f"E_psZ{i}", [128, 512], F32)) for i in range(2)]
        psR = ph.enter_context(nc.psum_tensor("E_psR", [1, 512], F32))

        tk_cs = [None]
        tk_cg = [None]

        def _(sy):
            sI.inc(sy.dma_start(out=idtf[:], in_=t["identf"]), 16)
            sI.inc(sy.dma_start(out=bf1row[:], in_=t["bf11"]), 16)
            with nc.allow_non_contiguous_dma("tiny one-time column loads"):
                sI.inc(sy.dma_start(out=g2c.ap().rearrange("p (c o) -> p c o", o=1),
                                    in_=t["g21"].rearrange("o (c p) -> p c o", p=128)), 16)
                tk_cs[0] = sI.inc(sy.dma_start(out=b2c.ap().rearrange("p (c o) -> p c o", o=1),
                                               in_=t["b21"].rearrange("o (c p) -> p c o", p=128)), 16)
        blk.sync(_)

        def _(gp):
            sIc.inc(gp.dma_start(out=wf1.ap().rearrange("p (c m) -> p c m", c=8),
                                 in_=t["wf11"].rearrange("(c p) m -> p c m", p=128)), 16)
            sIc.inc(gp.dma_start(out=wf2.ap().rearrange("p (uc m) -> p uc m", uc=32),
                                 in_=t["wf21"].rearrange("(uc p) m -> p uc m", p=128)), 16)
            sIc.inc(gp.dma_start(out=ones1[:], in_=t["ones1"]), 16)
            tk_cg[0] = sIc.inc(gp.dma_start(out=bf2e[:], in_=t["bf21"]), 16)
        blk.gpsimd(_)

        def _(ve):
            ve.memset(epsb[:], EPS)
            wt(ve, tk_cs[0])
            wt(ve, tk_cg[0])
            sVE.inc(ve.tensor_copy(b2cb[:], b2c[:]))
        blk.vector(_)
        tk_pre = sVE.tok()

        tk_brow = [None]
        for hf8 in range(8):
            def _(tn, hf8=hf8):
                wt(tn, tk_pre)
                wt(tn, tk_brow[0])
                i = None
                for c in range(NCH):
                    i = tn.matmul(psR[:], b2cb[:, c:c + 1],
                                  wf1[:, c * 4096 + hf8 * 512: c * 4096 + (hf8 + 1) * 512],
                                  start=(c == 0), stop=(c == NCH - 1))
                sMM.inc(i)
            blk.tensor(_)
            tbm = sMM.tok()

            def _(ve, hf8=hf8, tbm=tbm):
                wt(ve, tbm)
                tk_brow[0] = sVE.inc(ve.tensor_add(
                    bf1row[:, hf8 * 512:(hf8 + 1) * 512],
                    bf1row[:, hf8 * 512:(hf8 + 1) * 512], psR[:]))
                ve.drain()
            blk.vector(_)

        tk_bounce = [None]

        def _(sy):
            wt(sy, tk_brow[0])
            tk_bounce[0] = sems["dbg0"].inc(sy.dma_start(out=t["dscr"], in_=bf1row[:]), 16)
        blk.sync(_)

        def _(sy):
            wt(sy, tk_bounce[0])
            with nc.allow_non_contiguous_dma("tiny bias column load"):
                tk_bounce[0] = sems["dbg1"].inc(
                    sy.dma_start(out=bf1c.ap().rearrange("p (j o) -> p j o", o=1),
                                 in_=t["dscr"].rearrange("o (j p) -> p j o", p=128)), 16)
        blk.sync(_)

        def _(ve):
            wt(ve, tk_brow[0])
            i = None
            for c in range(NCH):
                i = ve.tensor_scalar(wf1[:, c * 4096:(c + 1) * 4096],
                                     wf1[:, c * 4096:(c + 1) * 4096],
                                     g2c[:, c:c + 1], None, ALU.mult)
            sVE.inc(i)
        blk.vector(_)
        tk_wsc = sVE.tok()

        tk_xi = [None]
        tk_x1 = [None]
        tk_store = [None, None]
        tk_store_last = [None]
        tk_ln = {}
        tk_tb = {}
        tk_gel = {}
        tk_hcons = [None]
        tk_zc = {}

        for g in range(NG):
            def _(sy, g=g):
                if g > 0:
                    wt(sy, tk_zc.get(0))
                    wt(sy, tk_zc.get(1))
                tk_xi[0] = sXI.inc(
                    sy.dma_start(out=xig.ap().rearrange("p (s d) -> p s d", s=4),
                                 in_=_xview(t["xi"], g)), 16)
            blk.sync(_)

            def _(ve, g=g):
                wt(ve, tk_xi[0])
                i = None
                for u in range(4):
                    col = slice(2 * u, 2 * u + 1)
                    ve.tensor_reduce(ssum[:, col], xig[:, u * 1024:(u + 1) * 1024],
                                     AX.X, ALU.add)
                    ve.drain()
                    i = ve.tensor_scalar(negm[:, col], ssum[:, col], -1.0 / D, None, ALU.mult)
                    ve.drain()
                sVE.inc(i)
            blk.vector(_)
            tvv = sVE.tok()

            def _(sc, tvv=tvv):
                wt(sc, tvv)
                i = None
                for u in range(4):
                    col = slice(2 * u, 2 * u + 1)
                    sc.activation(sq[u % 2][:], xig[:, u * 1024:(u + 1) * 1024],
                                  AF.Square, bias=negm[:, col], accum_out=ssq[:, col])
                    sc.drain()
                    sc.activation(lnv[:, col], ssq[:, col], AF.Ln,
                                  bias=epsb[:, 0:1], scale=1.0 / D)
                    sc.drain()
                    i = sc.activation(rstd[:, col], lnv[:, col], AF.Exp, scale=-0.5)
                    sc.drain()
                sEV.inc(i)
            blk.scalar(_)
            tss = sEV.tok()

            def _(ve, tss=tss):
                wt(ve, tss)
                i = None
                for u in range(4):
                    col = slice(2 * u, 2 * u + 1)
                    i = ve.tensor_mul(mrstd[:, col], negm[:, col], rstd[:, col])
                    ve.drain()
                sVE.inc(i)
            blk.vector(_)
            tmr = sVE.tok()

            aeb = sEV.v
            tvb = sVE.v
            tmb = sMM.v

            def _(sc, g=g, tmr=tmr, tvb=tvb):
                wt(sc, tmr)
                for u in range(4):
                    col = slice(2 * u, 2 * u + 1)
                    if u >= 2:
                        wt(sc, (sVE.h, tvb + 2 * (u - 2) + 2))
                    sEV.inc(sc.activation(lnscr[u % 2][:], xig[:, u * 1024:(u + 1) * 1024],
                                          AF.Identity, bias=mrstd[:, col],
                                          scale=rstd[:, col]))
            blk.scalar(_)

            def _(tn, g=g, aeb=aeb, tvb=tvb):
                if g == 0:
                    wt(tn, tk_cs[0])
                for u in range(4):
                    wt(tn, (sEV.h, aeb + u + 1))
                    for hf2 in range(2):
                        idx = u * 2 + hf2
                        wt(tn, (sVE.h, max(0, tvb + idx - 1)))
                        i = None
                        for cc in range(4):
                            ch = hf2 * 4 + cc
                            i = tn.transpose(psT[hf2][:, cc * 128:(cc + 1) * 128],
                                             lnscr[u % 2][:, ch * 128:(ch + 1) * 128],
                                             idtf[:])
                        sMM.inc(i)
            blk.tensor(_)

            def _(ve, g=g, tmb=tmb):
                for u in range(4):
                    for hf2 in range(2):
                        idx = u * 2 + hf2
                        wt(ve, (sMM.h, tmb + idx + 1))
                        dst = h2T.ap().rearrange("p (c q) -> p c q", c=8)[
                            :, hf2 * 4:(hf2 + 1) * 4, u * 128:(u + 1) * 128]
                        sVE.inc(ve.tensor_copy(dst, psT[hf2].ap()
                                               .rearrange("p (c q) -> p c q", c=4)))
            blk.vector(_)
            tk_h2T = sVE.tok()

            for th in range(2):
                t0 = th * 256
                fmb = sMM.v
                geb = sEV.v

                def _(tn, th=th, t0=t0, geb=geb, g=g):
                    if th == 0:
                        wt(tn, tk_wsc)
                        wt(tn, tk_h2T)
                    for uch in range(32):
                        bank = uch % 2
                        if uch >= 2:
                            wt(tn, (sEV.h, geb + uch - 1))
                        elif tk_hcons[0] is not None:
                            wt(tn, tk_hcons[0])
                        i = None
                        for c in range(NCH):
                            i = tn.matmul(psA[bank][:, 0:256],
                                          wf1[:, c * 4096 + uch * 128:
                                              c * 4096 + (uch + 1) * 128],
                                          h2T[:, c * 512 + t0: c * 512 + t0 + 256],
                                          start=(c == 0), stop=(c == NCH - 1))
                        sMM.inc(i)
                blk.tensor(_)

                def _(sc, th=th, fmb=fmb, g=g):
                    for uch in range(32):
                        bank = uch % 2
                        wt(sc, (sMM.h, fmb + uch + 1))
                        if uch < 2 and tk_hcons[0] is not None:
                            wt(sc, tk_hcons[0])
                        if g == 0 and th == 0 and uch < 2:
                            wt(sc, tk_bounce[0])
                        sEV.inc(sc.activation(hf[:, uch * 256:(uch + 1) * 256],
                                              psA[bank][:, 0:256], AF.Gelu,
                                              bias=bf1c[:, uch:uch + 1]))
                blk.scalar(_)
                tk_f1 = sEV.tok()

                def _(sy, g=g, th=th):
                    wt(sy, tk_store_last[0])
                    tk_x1[0] = sX1.inc(sy.dma_start(
                        out=x1t.ap().rearrange("p (s d) -> p s d", s=2),
                        in_=t["x1"][g * GT + th * 256: g * GT + (th + 1) * 256, :]
                        .rearrange("(s p) d -> p s d", p=128)), 16)
                blk.sync(_)

                zmb = sMM.v
                zvb = sVE.v

                def _(tn, th=th, tk_f1=tk_f1, zvb=zvb):
                    wt(tn, tk_f1)
                    for idx in range(4):
                        tsub, dh = idx // 2, idx % 2
                        zb = idx % 2
                        if idx >= 2:
                            wt(tn, (sVE.h, zvb + (idx - 2) + 1))
                        else:
                            wt(tn, tk_zc.get(zb))
                        i = None
                        for uc in range(32):
                            i = tn.matmul(psZ[zb][:],
                                          hf[:, uc * 256 + tsub * 128:
                                             uc * 256 + (tsub + 1) * 128],
                                          wf2[:, uc * 1024 + dh * 512:
                                              uc * 1024 + (dh + 1) * 512],
                                          start=(uc == 0), stop=False)
                        i = tn.matmul(psZ[zb][:], ones1[:, 0:128],
                                      bf2e[:, dh * 512:(dh + 1) * 512],
                                      start=False, stop=True)
                        tk_hcons[0] = sMM.inc(i)
                blk.tensor(_)

                def _(ve, th=th, zmb=zmb):
                    for idx in range(4):
                        tsub, dh = idx // 2, idx % 2
                        zb = idx % 2
                        wt(ve, (sMM.h, zmb + idx + 1))
                        if idx == 0:
                            wt(ve, tk_x1[0])
                        slx = slice(tsub * 1024 + dh * 512, tsub * 1024 + (dh + 1) * 512)
                        sli = slice(th * 2048 + tsub * 1024 + dh * 512,
                                    th * 2048 + tsub * 1024 + (dh + 1) * 512)
                        ve.tensor_add(x1t[:, slx], x1t[:, slx], psZ[zb][:])
                        ve.drain()
                        tk_zc[zb] = sVE.inc(
                            ve.tensor_add(x1t[:, slx], x1t[:, slx], xig[:, sli]))
                        ve.drain()
                blk.vector(_)

                def _(sy, g=g, th=th):
                    wt(sy, tk_zc[0])
                    wt(sy, tk_zc[1])
                    tk_store[(g * 2 + th) % 2] = tk_store_last[0] = sSTp[(g * 2 + th) % 2].inc(
                        sy.dma_start(
                            out=t["xab"][g * GT + th * 256: g * GT + (th + 1) * 256, :]
                            .rearrange("(s p) d -> p s d", p=128),
                            in_=x1t.ap().rearrange("p (s d) -> p s d", s=2)), 16)
                blk.sync(_)

        def _(sy):
            wt(sy, tk_store[0])
            wt(sy, tk_store[1])
        blk.sync(_)



def phase_F(nc, st, sems, t):
    sMM, sEV, sVE = (sems[k] for k in ["mm", "ev", "ve"])
    sIc = sems["ld4"]
    sXB = sems["ld1"]
    sXN = sems["ld2"]
    sI = sems["ld3"]
    sST = sems["st"]

    with ExitStack() as ph, nc.Block("phF") as blk:
        def sb(name, shape, dt):
            return ph.enter_context(nc.sbuf_tensor(name, shape, dt))

        wfc = sb("F_wfc", [128, 8 * 4096], BF16)
        wpj = sb("F_wpj", [128, 8 * 4096], BF16)
        m1 = sb("F_m1", [128, 32 * 8], F32)
        xnTg = sb("F_xnTg", [128, 8 * 512], BF16)
        yac = sb("F_yac", [128, 4 * 1024], F32)
        scr = sb("F_scr", [128, 2 * 512], F32)
        hq = sb("F_hq", [128, 4 * 512], BF16)
        hs = sb("F_hs", [128, 2 * 512], BF16)

        psH = [ph.enter_context(nc.psum_tensor(f"F_psH{i}", [128, 512], F32)) for i in range(4)]
        psZ = [ph.enter_context(nc.psum_tensor(f"F_psZ{i}", [128, 512], F32)) for i in range(2)]

        tk_c = [None]

        def _(gp):
            sIc.inc(gp.dma_start(
                out=wfc.ap().rearrange("p (e c u) -> p e c u", e=8, c=8),
                in_=t["wfcT"].rearrange("e (c p) u -> p e c u", p=128)), 16)
            tk_c[0] = sIc.inc(gp.dma_start(
                out=wpj.ap().rearrange("p (e uc d) -> p e uc d", e=8, uc=4),
                in_=t["wprojT"].rearrange("e (uc p) d -> p e uc d", p=128)), 16)
        blk.gpsimd(_)

        tk_m = [None]

        def _(sy):
            tk_m[0] = sI.inc(sy.dma_start(out=m1[:], in_=t["m1"]), 16)
        blk.sync(_)

        tk_xn = [None]
        tk_y = [None]
        tk_ys = [None]
        tk_act = {}
        tk_hu = {}
        tk_z = {}
        tk_xnc = [None]
        tk_prjd = [None]

        for g in range(NG):
            def _(sy, g=g):
                if g > 0:
                    wt(sy, tk_ys[0])
                tk_y[0] = sXB.inc(
                    sy.dma_start(out=yac.ap().rearrange("p (s d) -> p s d", s=4),
                                 in_=_xview(t["xab"], g)), 16)
            blk.sync(_)

            def _(sy, g=g):
                if g > 0:
                    wt(sy, tk_xnc[0])
                tk_xn[0] = sXN.inc(
                    sy.dma_start(out=xnTg.ap().rearrange("p (c q) -> p c q", c=8),
                                 in_=_xnT_view(t["xn1T"], g)), 16)
            blk.sync(_)

            fmb_of = {}
            svb_of = {}
            seb_of = {}

            def emit_fc(e):
                fmb_of[e] = sMM.v
                svb_pred = sVE.v + (8 if e > 0 else 0)

                def _(tn, e=e):
                    if e == 0:
                        wt(tn, tk_xn[0])
                        wt(tn, tk_c[0])
                    for uch in range(4):
                        bank = uch
                        if e > 0:
                            wt(tn, (sVE.h, svb_of[e - 1] + uch + 1))
                        elif tk_act.get(bank) is not None:
                            wt(tn, tk_act[bank])
                        i = None
                        for c in range(NCH):
                            i = tn.matmul(psH[bank][:],
                                          wfc[:, e * 4096 + c * 512 + uch * 128:
                                              e * 4096 + c * 512 + (uch + 1) * 128],
                                          xnTg[:, c * 512:(c + 1) * 512],
                                          start=(c == 0), stop=(c == NCH - 1))
                        sMM.inc(i)
                blk.tensor(_)

            def emit_act(e):
                seb = sEV.v
                svb = sVE.v
                seb_of[e] = seb
                svb_of[e] = svb
                fmb = fmb_of[e]

                def _(sc, e=e, fmb=fmb, svb=svb, seb=seb):
                    for uch in range(4):
                        sb2 = uch % 2
                        wt(sc, (sMM.h, fmb + uch + 1))
                        if uch >= 2:
                            wt(sc, (sVE.h, svb + (uch - 2) + 1))
                        elif e > 0:
                            wt(sc, (sVE.h, svb_of[e - 1] + uch + 3))
                        sEV.inc(sc.activation(scr[:, sb2 * 512:(sb2 + 1) * 512],
                                               psH[uch][:], AF.Abs, scale=0.25))
                        sc.drain()
                    for uch in range(4):
                        sb2 = uch % 2
                        wt(sc, (sVE.h, svb + uch + 1))
                        wt(sc, tk_prjd[0])
                        sEV.inc(sc.activation(hq[:, uch * 512:(uch + 1) * 512],
                                              hs[:, sb2 * 512:(sb2 + 1) * 512],
                                              AF.Square))
                        sc.drain()
                blk.scalar(_)

                def _(ve, e=e, fmb=fmb, seb=seb):
                    for uch in range(4):
                        sb2 = uch % 2
                        wt(ve, (sEV.h, seb + uch + 1))
                        if uch >= 2:
                            wt(ve, (sEV.h, seb + 4 + (uch - 2) + 1))
                        elif e > 0:
                            wt(ve, (sEV.h, seb_of[e - 1] + 4 + uch + 3))
                        tk_act[uch] = sVE.inc(ve.scalar_tensor_tensor(
                            hs[:, sb2 * 512:(sb2 + 1) * 512], psH[uch][:], 0.75,
                            scr[:, sb2 * 512:(sb2 + 1) * 512], ALU.mult, ALU.add))
                        ve.drain()
                blk.vector(_)

            def emit_proj(e):
                tk_hq = (sEV.h, seb_of[e] + 8)
                pmb = sMM.v
                pvb = sVE.v

                def _(tn, e=e, tk_hq=tk_hq, pvb=pvb):
                    wt(tn, tk_hq)
                    for idx in range(8):
                        tsub, dh = idx // 2, idx % 2
                        zb = idx % 2
                        if idx >= 2:
                            wt(tn, (sVE.h, pvb + (idx - 2) + 1))
                        else:
                            wt(tn, tk_z.get(zb))
                        i = None
                        for uc in range(4):
                            i = tn.matmul(psZ[zb][:],
                                          hq[:, uc * 512 + tsub * 128:
                                             uc * 512 + (tsub + 1) * 128],
                                          wpj[:, e * 4096 + uc * 1024 + dh * 512:
                                              e * 4096 + uc * 1024 + (dh + 1) * 512],
                                          start=(uc == 0), stop=(uc == 3))
                        sMM.inc(i)
                blk.tensor(_)
                tk_prjd[0] = sMM.tok()

                def _(ve, e=e, g=g, pmb=pmb):
                    for idx in range(8):
                        tsub, dh = idx // 2, idx % 2
                        zb = idx % 2
                        wt(ve, (sMM.h, pmb + idx + 1))
                        if e == 0 and idx == 0:
                            wt(ve, tk_y[0])
                            wt(ve, tk_m[0])
                        sl = slice(tsub * 1024 + dh * 512, tsub * 1024 + (dh + 1) * 512)
                        ti = g * 4 + tsub
                        tk_z[zb] = sVE.inc(ve.scalar_tensor_tensor(
                            yac[:, sl], psZ[zb][:],
                            m1[:, ti * 8 + e: ti * 8 + e + 1],
                            yac[:, sl], ALU.mult, ALU.add))
                        ve.drain()
                blk.vector(_)

            emit_fc(0)
            for e in range(E1):
                emit_act(e)
                if e + 1 < E1:
                    emit_fc(e + 1)
                emit_proj(e)
                if e == E1 - 1:
                    tk_xnc[0] = tk_prjd[0]

            def _(sy, g=g):
                wt(sy, tk_z[0])
                wt(sy, tk_z[1])
                tk_ys[0] = sST.inc(
                    sy.dma_start(out=_xview(t["out"], g),
                                 in_=yac.ap().rearrange("p (s d) -> p s d", s=4)), 16)
            blk.sync(_)

        def _(sy):
            wt(sy, tk_ys[0])
        blk.sync(_)


PHASE_BUILDERS = {"A": phase_A, "B": phase_B, "C": phase_C, "D": phase_D,
                  "E": phase_E, "F": phase_F}


def build_program(phases, fused):
    tk_lr_prev[0] = None
    tk_router_done[0] = None
    tk_tbank[0] = tk_tbank[1] = None
    tk_psL_B[0] = None
    tk_epi_cons.clear()
    tk_epi_dbg.clear()
    nc = bass.Bass("TRN2", target_bir_lowering=False, debug=False)
    st = ExitStack()
    with st:
        t = declare_drams(nc, phases, fused)
        sems = {n: S(nc, st, n) for n in
                ["ld0", "ld1", "ld2", "ld3", "ld4", "ld5", "mm", "ev", "ve",
                 "gp", "st", "st2", "dbg0", "dbg1"]}
        for ph in phases:
            PHASE_BUILDERS[ph](nc, st, sems, t)
    return nc


FUSED = [True]
PHASE_ORDER = ["A", "B", "C", "D", "E", "F"]


def kernel(**inputs):
    w, c = host_prep(inputs)
    x = np.asarray(inputs["L0_attn_x"], dtype=np.float32)
    base = {}
    base.update({k: w[k] for k in WEIGHT_SHAPES})
    base.update({k: c[k] for k in CONST_SHAPES})

    if FUSED[0]:
        nc = build_program(PHASE_ORDER, fused=True)
        needed = set()
        for phn in PHASE_ORDER:
            needed |= set(PHASE_IN[phn])
        in_maps = []
        for bcore in range(NC):
            m = {k: v for k, v in base.items() if k in needed}
            m["x"] = np.ascontiguousarray(x[bcore])
            in_maps.append(m)
        res = run_bass_kernel_spmd(nc, in_maps, core_ids=list(range(NC)))
        return np.stack([np.asarray(r["out"], dtype=np.float32) for r in res.results])

    state = [{"x": np.ascontiguousarray(x[bcore])} for bcore in range(NC)]
    for phn in PHASE_ORDER:
        nc = build_program([phn], fused=False)
        in_maps = []
        for bcore in range(NC):
            m = {}
            for k in PHASE_IN[phn]:
                if k in base:
                    m[k] = base[k]
                else:
                    m[k] = state[bcore][k]
            in_maps.append(m)
        res = run_bass_kernel_spmd(nc, in_maps, core_ids=list(range(NC)))
        for bcore in range(NC):
            for k in PHASE_OUT[phn]:
                state[bcore][k] = np.asarray(res.results[bcore][k])
    return np.stack([state[bcore]["out"].astype(np.float32) for bcore in range(NC)])
